# version 16
# baseline (speedup 1.0000x reference)
import numpy as np
import ml_dtypes
import concourse.bass as bass
import concourse.mybir as mybir
from concourse.bass_utils import run_bass_kernel_spmd

F32 = mybir.dt.float32
BF16 = mybir.dt.bfloat16
I32 = mybir.dt.int32
AF = mybir.ActivationFunctionType
ALU = mybir.AluOpType
BF = ml_dtypes.bfloat16

D = 4096
T = 2048
NCH = 32
NTT = 4
TT = 512


def lhsT_blocked(wt: np.ndarray) -> np.ndarray:
    d, o = wt.shape
    ndc, noc = d // 128, o // 128
    return np.ascontiguousarray(
        wt.reshape(ndc, 128, noc, 128).transpose(2, 1, 0, 3).reshape(noc, 128, ndc * 128))


def fm_cols(v: np.ndarray) -> np.ndarray:
    return np.ascontiguousarray(v.reshape(-1, 128).T)



ENGINES = ("tensor", "vector", "scalar", "gpsimd", "sync")


class Sem:
    __slots__ = ("name", "h", "n")

    def __init__(self, name):
        self.name, self.h, self.n = name, None, 0


class _RecInst:
    __slots__ = ("item",)

    def __init__(self, item):
        self.item = item

    def then_inc(self, sem: Sem, k: int):
        sem.n += k
        self.item[4].append((sem, k))
        return self


class _RecEng:
    def __init__(self, prog, name):
        self._prog, self._name = prog, name

    def __getattr__(self, meth):
        prog, name = self._prog, self._name

        def call(*a, **kw):
            item = [name, meth, a, kw, []]
            prog.ops.append(item)
            return _RecInst(item)
        return call

    def wait(self, sem: Sem, val=None):
        v = sem.n if val is None else val
        if v > 0:
            self._prog.ops.append([self._name, "wait_ge", (sem, v), {}, []])


class Prog:
    def __init__(self):
        self.ops = []
        self.t = _RecEng(self, "tensor")
        self.v = _RecEng(self, "vector")
        self.s = _RecEng(self, "scalar")
        self.g = _RecEng(self, "gpsimd")
        self.y = _RecEng(self, "sync")


def replay(P: Prog, name, eng):
    for e, meth, a, kw, incs in P.ops:
        if e != name:
            continue
        if meth == "wait_ge":
            sem, v = a
            eng.wait_ge(sem.h, v)
            continue
        inst = getattr(eng, meth)(*a, **kw)
        for sem, k in incs:
            inst = inst.then_inc(sem.h, k)



class Ctx:
    pass


def build_nc(sublayers, weights_meta, dump=None):
    nc = bass.Bass("TRN2", target_bir_lowering=False, debug=False)
    C = Ctx()
    C.nc = nc
    P = Prog()
    C.P = P
    C.dump = dump

    C.dram = {}
    for name, (shape, dt) in weights_meta.items():
        C.dram[name] = nc.dram_tensor(name, list(shape), dt, kind="ExternalInput").ap()
    C.out = nc.dram_tensor("out", [D, T], F32, kind="ExternalOutput").ap()
    C.scr_xn = nc.dram_tensor("scr_xn", [128, NCH * T], BF16).ap()
    C.scr_q = nc.dram_tensor("scr_q", [128, NCH * T], BF16).ap()
    C.scr_attn = nc.dram_tensor("scr_attn", [128, NCH * T], BF16).ap()
    C.scr_kv = nc.dram_tensor("scr_kv", [128, 24 * T], BF16).ap()
    C.scr_w1 = nc.dram_tensor("scr_w1", [NCH, 128, NCH * 128], BF16).ap()
    C.scr_misc = nc.dram_tensor("scr_misc", [1, 65536], F32).ap()
    if dump is not None:
        C.out_dbg = nc.dram_tensor("out_dbg", [128, 65536], BF16,
                                   kind="ExternalOutput").ap()

    C.sem = {s: Sem(s) for s in ("ldx", "ldw", "lds", "st", "pe", "ac", "dv", "pl")}
    C.xbf_marks = {}

    import contextlib
    stack = contextlib.ExitStack()
    with stack:
        sb = {}

        def alloc(name, shape, dt):
            sb[name] = stack.enter_context(nc.sbuf_tensor(name, shape, dt))

        alloc("BIG", [128, 65536], BF16)
        alloc("MED", [128, 8192], BF16)
        alloc("WB", [128, 8704], BF16)
        alloc("BF32", [128, 7168], F32)
        alloc("SF32", [128, 512], F32)
        alloc("PRM", [128, 1280], BF16)
        alloc("ROWS", [1, 3072], F32)
        alloc("ROWSB", [1, 512], BF16)
        C.sb = sb

        ps = {}
        for i in range(6):
            ps[i] = stack.enter_context(nc.psum_tensor(f"ps{i}", [128, 512], F32))
        ps[6] = stack.enter_context(nc.psum_tensor("ps6b", [128, 1024], BF16))
        ps[7] = stack.enter_context(nc.psum_tensor("ps7b", [128, 1024], BF16))
        C.ps = ps

        for s in C.sem:
            C.sem[s].h = stack.enter_context(nc.semaphore(s))

        C.ident = sb["PRM"][:, 0:128]
        C.prot = sb["PRM"][:, 128:256]
        C.mask896 = sb["PRM"][:, 256:1152]
        C.ones128 = sb["PRM"][:, 1152:1153]
        C.onesrow_b = sb["ROWSB"][:, 0:512]
        C.onesrow_f = sb["ROWS"][:, 0:512]

        emit_prologue(C)
        for i, sl in enumerate(sublayers):
            EMITTERS[sl](C, final=(i == len(sublayers) - 1))
        emit_epilogue(C)

        with nc.Block() as block:
            for eng_name in ENGINES:
                def mk(nm):
                    def body(eng):
                        replay(P, nm, eng)
                    return body
                getattr(block, eng_name)(mk(eng_name))
    return nc



def emit_prologue(C):
    P, sb, dram, S, ps = C.P, C.sb, C.dram, C.sem, C.ps
    for dst, src in [
        (sb["PRM"][:, 0:128], "c_ident"),
        (sb["PRM"][:, 128:256], "c_prot"),
        (sb["PRM"][:, 256:1152], "c_mask896"),
        (sb["PRM"][:, 1152:1153], "c_ones128"),
        (sb["ROWSB"][:, 0:512], "c_onesb"),
        (sb["ROWS"][:, 0:512], "c_onesf"),
        (sb["SF32"][:, 0:32], "l0f_bias"),
        (sb["SF32"][:, 32:64], "l1f_bias"),
        (sb["SF32"][:, 64:96], "l1f_scale"),
        (sb["SF32"][:, 96:128], "l1f_zp"),
        (sb["SF32"][0:16, 132:388], "c_sel"),
        (sb["SF32"][0:16, 130:131], "l0f_coeffs"),
        (sb["SF32"][0:16, 131:132], "l0f_scales"),
    ]:
        P.y.dma_start(out=dst, in_=dram[src]).then_inc(S["ldx"], 16)
    P.v.wait(S["ldx"])
    P.v.tensor_tensor(out=sb["SF32"][0:16, 130:131], in0=sb["SF32"][0:16, 130:131],
                      in1=sb["SF32"][0:16, 131:132], op=ALU.mult).then_inc(S["dv"], 1)
    P.t.wait(S["dv"])
    P.t.matmul(ps[4][:, 0:1], sb["SF32"][0:16, 132:260], sb["SF32"][0:16, 130:131],
               start=True, stop=True)
    mm = P.t.matmul(ps[4][:, 1:2], sb["SF32"][0:16, 260:388], sb["SF32"][0:16, 130:131],
                    start=True, stop=True)
    mm.then_inc(S["pe"], 1)
    P.v.wait(S["pe"])
    P.v.tensor_copy(out=sb["SF32"][:, 128:130], in_=ps[4][:, 0:2]).then_inc(S["dv"], 1)


def emit_epilogue(C):
    P, S = C.P, C.sem
    P.y.wait(S["st"])
    P.y.wait(S["lds"])
    P.y.wait(S["ldw"])
    if getattr(C, "dump", None) is not None:
        P.y.wait(S["dv"])
        P.y.wait(S["ac"])
        P.y.wait(S["pe"])
        if C.dump == "scr_misc":
            P.g.wait(S["st"])
            P.g.wait(S["lds"])
            P.g.dma_start(out=C.out_dbg[0:1, 0:65536], in_=C.scr_misc
                          ).then_inc(S["ldx"], 16)
        else:
            src = {"scr_q": C.scr_q, "scr_kv": C.scr_kv, "scr_attn": C.scr_attn,
                   "scr_xn": C.scr_xn}[C.dump]
            n = src.shape[1]
            P.y.dma_start(out=C.out_dbg[:, 0:n], in_=src).then_inc(S["ldx"], 16)
    P.y.wait(S["ldx"])



def ln_into_big(C, x_src):
    P, sb, S, ps = C.P, C.sb, C.sem, C.ps
    xr = x_src.rearrange("(c p) t -> p c t", p=128)
    bigr = sb["BIG"][:, 0:65536].rearrange("p (c t) -> p c t", t=T)
    pe_entry = S["pe"].n
    dv_entry = S["dv"].n
    C.ln_dv_entry = dv_entry
    C.ln_ac_entry = S["ac"].n
    xbfr = C.scr_xn.rearrange("p (c t) -> p c t", t=T)
    P.y.wait(S["pe"], pe_entry)
    P.y.wait(S["dv"], dv_entry)
    ldx_of = []
    for cb in range(8):
        P.y.wait(S["lds"], C.xbf_marks.get(4 * cb + 3, 0))
        P.y.dma_start(out=bigr[:, 4 * cb:4 * cb + 4, :],
                      in_=xbfr[:, 4 * cb:4 * cb + 4, :]).then_inc(S["ldx"], 16)
        ldx_of.append(S["ldx"].n)
    P.s.wait(S["pe"], pe_entry)
    P.s.wait(S["lds"], getattr(C, "med_lds", 0))
    x2_rd = [pe_entry, pe_entry]
    C.ln_tt_dv = []
    rb_of = [sb["BF32"][:, 4096:4608], sb["BF32"][:, 5120:5632]]
    mrb_of = [sb["BF32"][:, 4608:5120], sb["BF32"][:, 5632:6144]]
    bank_free = [(S["dv"], dv_entry), (S["dv"], dv_entry)]
    stat_pe = [0] * NTT

    def emit_stats(tt):
        par = tt % 2
        pA, pB = (ps[4], ps[5]) if par == 0 else (ps[2], ps[3])
        P.t.wait(*bank_free[par])
        for cb in range(8):
            sl = cb % 2
            x2 = sb["MED"][:, 4096 + sl * 2048:4096 + (sl + 1) * 2048]
            x23 = x2.rearrange("p (c t) -> p c t", t=TT)
            P.s.wait(S["ldx"], ldx_of[cb])
            P.s.wait(S["pe"], x2_rd[sl])
            P.s.activation(out=x23, in_=bigr[:, 4 * cb:4 * cb + 4,
                                            tt * TT:(tt + 1) * TT],
                           func=AF.Square).then_inc(S["ac"], 1)
            if tt == 0:
                P.t.wait(S["ldx"], ldx_of[cb])
            first = (cb == 0)
            for k in range(4):
                P.t.matmul(pA[0:1, :], C.ones128,
                           xn_ap(C, 4 * cb + k, tt * TT, (tt + 1) * TT),
                           start=(first and k == 0), stop=(cb == 7 and k == 3))
            P.t.wait(S["ac"])
            for k in range(4):
                mm = P.t.matmul(pB[0:1, :], C.ones128, x2[:, k * TT:(k + 1) * TT],
                                start=(first and k == 0), stop=(cb == 7 and k == 3))
            mm.then_inc(S["pe"], 1)
            x2_rd[sl] = S["pe"].n
        stat_pe[tt] = S["pe"].n

    emit_stats(0)
    last_norm_par = [[], []]
    rows_done = [None] * NTT

    def emit_rows(tt, prev_pe_bc=0, prev_rows_dv=0):
        pA, pB = (ps[4], ps[5]) if tt % 2 == 0 else (ps[2], ps[3])
        r_mean = sb["ROWS"][:, 512:1024]
        r_var = sb["ROWS"][:, 1024:1536]
        r_m2 = sb["ROWS"][:, 1536:2048]
        r_rstd = sb["ROWS"][:, 2048:2560]
        r_mr = sb["ROWS"][:, 2560:3072]
        if prev_rows_dv > 0:
            P.s.wait(S["dv"], prev_rows_dv)
        if prev_pe_bc > 0:
            P.s.wait(S["pe"], prev_pe_bc)
        P.s.wait(S["pe"], stat_pe[tt])
        P.s.activation(out=r_mean, in_=pA[0:1, :], func=AF.Copy,
                       scale=np.float32(1.0 / D)).then_inc(S["ac"], 1)
        P.s.activation(out=r_var, in_=pB[0:1, :], func=AF.Copy,
                       scale=np.float32(1.0 / D)).then_inc(S["ac"], 1)
        P.s.activation(out=r_m2, in_=r_mean, func=AF.Square).then_inc(S["ac"], 1)
        P.v.wait(S["ac"], S["ac"].n)
        P.v.tensor_tensor(out=r_var, in0=r_var, in1=r_m2, op=ALU.subtract
                          ).then_inc(S["dv"], 1)
        P.s.wait(S["dv"], S["dv"].n)
        P.s.activation(out=r_var, in_=r_var, func=AF.Ln, bias=np.float32(1e-5)).then_inc(S["ac"], 1)
        P.s.activation(out=r_rstd, in_=r_var, func=AF.Exp, scale=np.float32(-0.5)
                       ).then_inc(S["ac"], 1)
        ac_rstd = S["ac"].n
        P.v.wait(S["ac"], ac_rstd)
        P.v.tensor_tensor(out=r_mr, in0=r_mean, in1=r_rstd, op=ALU.mult
                          ).then_inc(S["dv"], 1)
        rows_done[tt] = (ac_rstd, S["dv"].n, r_rstd, r_mr)

    emit_rows(0)
    for tt in range(NTT):
        par = tt % 2
        pA, pB = (ps[4], ps[5]) if par == 0 else (ps[2], ps[3])
        if tt + 1 < NTT:
            emit_stats(tt + 1)
        ac_rstd, rows_dv, r_rstd, r_mr = rows_done[tt]
        P.t.wait(S["ac"], ac_rstd)
        P.t.wait(S["dv"], rows_dv)
        P.t.matmul(pA[:, :], C.onesrow_f[:, 0:128], r_rstd, start=True, stop=True)
        mm = P.t.matmul(pB[:, :], C.onesrow_f[:, 0:128], r_mr, start=True, stop=True)
        mm.then_inc(S["pe"], 1)
        pe_bc = S["pe"].n
        P.s.wait(S["pe"], pe_bc)
        for sem, val in last_norm_par[par]:
            P.s.wait(sem, val)
        P.s.copy(out=rb_of[par], in_=pA[:, :]).then_inc(S["ac"], 1)
        P.s.copy(out=mrb_of[par], in_=pB[:, :]).then_inc(S["ac"], 1)
        evac_ac = S["ac"].n
        if tt + 1 < NTT:
            emit_rows(tt + 1, pe_bc, rows_dv)
        bank_free[par] = (S["ac"], evac_ac)
        t1v = sb["BF32"][:, 6144:6656]
        t1g = sb["BF32"][:, 6656:7168]
        first_v, first_g = True, True
        for c in range(NCH):
            xv = xn_ap(C, c, tt * TT, (tt + 1) * TT)
            if c % 4 != 3:
                if first_v:
                    P.v.wait(S["ac"], evac_ac)
                    P.v.wait(S["ldx"], ldx_of[7])
                    P.v.wait(S["pe"], pe_bc)
                    first_v = False
                P.v.tensor_tensor(out=t1v, in0=xv, in1=rb_of[par], op=ALU.mult
                                  ).then_inc(S["dv"], 1)
                P.v.tensor_tensor(out=xv, in0=t1v, in1=mrb_of[par], op=ALU.subtract
                                  ).then_inc(S["dv"], 1)
            else:
                if first_g:
                    P.g.wait(S["ac"], evac_ac)
                    P.g.wait(S["ldx"], ldx_of[7])
                    P.g.wait(S["pe"], pe_bc)
                    first_g = False
                P.g.tensor_tensor(out=t1g, in0=xv, in1=rb_of[par], op=ALU.mult
                                  ).then_inc(S["pl"], 1)
                P.g.tensor_tensor(out=xv, in0=t1g, in1=mrb_of[par], op=ALU.subtract
                                  ).then_inc(S["pl"], 1)
        last_norm_par[par] = [(S["dv"], S["dv"].n), (S["pl"], S["pl"].n)]
        C.ln_tt_dv.append(list(last_norm_par[par]))


def xn_ap(C, c, t0, t1):
    return C.sb["BIG"][:, c * T + t0:c * T + t1]



def preload_w(C, wname, wcols, noc, wq_engine="g"):
    P, sb, S = C.P, C.sb, C.sem
    if not hasattr(C, "wb_mark"):
        C.wb_mark = [0, 0]
    if getattr(C, "wb_wcols", None) != wcols:
        m = max(C.wb_mark)
        C.wb_mark = [m, m]
        C.wb_wcols = wcols
    weng = getattr(P, wq_engine)
    wv_of = {}
    for oc in range(min(2, noc)):
        sl = oc % 2
        weng.wait(S["pe"], C.wb_mark[sl])
        weng.dma_start(out=sb["WB"][:, sl * wcols:(sl + 1) * wcols],
                       in_=C.dram[wname][oc] if isinstance(wname, str) else wname[oc]
                       ).then_inc(S["ldw"], 16)
        wv_of[oc] = S["ldw"].n
    return wv_of


def proj_loop(C, wname, rhs_fn, noc, wcols, terms_fn, epi_fn, cast=True,
              wq_engine="g", first_waits=(), tt_gates=None, pre=None):
    P, sb, S, ps = C.P, C.sb, C.sem, C.ps
    if not hasattr(C, "wb_mark"):
        C.wb_mark = [0, 0]
    if getattr(C, "wb_wcols", None) != wcols:
        m = max(C.wb_mark)
        C.wb_mark = [m, m]
        C.wb_wcols = wcols
    if tt_gates is not None:
        g0 = (S["dv"], C.ln_dv_entry)
        a0 = (S["ac"], C.ln_ac_entry)
    else:
        g0 = (S["dv"], S["dv"].n)
        a0 = (S["ac"], S["ac"].n)
    bank_gate = [g0, g0, g0, g0]
    first_gate_extra = a0
    weng = getattr(P, wq_engine)
    wsem = S["ldw"]
    wv_of = {}
    if pre is not None:
        wv_of.update(pre)

    def emit_load(oc):
        sl = oc % 2
        weng.wait(S["pe"], C.wb_mark[sl])
        weng.dma_start(out=sb["WB"][:, sl * wcols:(sl + 1) * wcols],
                       in_=C.dram[wname][oc] if isinstance(wname, str) else wname[oc]
                       ).then_inc(wsem, 16)
        wv_of[oc] = wsem.n

    if 0 not in wv_of:
        emit_load(0)
    if noc > 1 and 1 not in wv_of:
        emit_load(1)
    flush = getattr(epi_fn, "flush", None)
    for oc in range(noc):
        sl = oc % 2
        for tt in range(NTT):
            g = oc * NTT + tt
            b = g % 4
            if tt == 0:
                P.t.wait(wsem, wv_of[oc])
            if oc == 0 and tt == 0:
                if first_gate_extra[1] > 0:
                    P.t.wait(*first_gate_extra)
                for fw in first_waits:
                    P.t.wait(*fw)
            if oc == 0 and tt_gates is not None:
                for sem, val in tt_gates[tt]:
                    P.t.wait(sem, val)
            if bank_gate[b] is not None and bank_gate[b][1] > 0:
                P.t.wait(*bank_gate[b])
            wb = sb["WB"][:, sl * wcols:(sl + 1) * wcols]
            pairs = terms_fn(oc, tt, wb)
            for j, (lh, rh) in enumerate(pairs):
                mm = P.t.matmul(ps[b][:, :], lh, rh,
                                start=(j == 0), stop=(j == len(pairs) - 1))
            mm.then_inc(S["pe"], 1)
            if tt == NTT - 1:
                C.wb_mark[sl] = S["pe"].n
            bank_gate[b] = epi_fn(oc, tt, ps[b][:, :])
        if oc + 2 < noc:
            emit_load(oc + 2)
        if flush is not None:
            flush(oc)


def terms_plain(C, xsrc_fn):
    def f(oc, tt, wb):
        return [(wb[:, dc * 128:(dc + 1) * 128], xsrc_fn(dc, tt * TT, (tt + 1) * TT))
                for dc in range(NCH)]
    return f


def terms_lowrank(C, xsrc_fn, xa_sb):
    def f(oc, tt, wb):
        pairs = [(wb[:, dc * 128:(dc + 1) * 128], xsrc_fn(dc, tt * TT, (tt + 1) * TT))
                 for dc in range(NCH)]
        for kc in range(2):
            pairs.append((wb[:, 4096 + kc * 128:4096 + (kc + 1) * 128],
                          xa_sb[:, kc * T + tt * TT:kc * T + (tt + 1) * TT]))
        return pairs
    return f


def epi_accum(C, bias_fn):
    P, sb, S = C.P, C.sb, C.sem
    st_mark = [0, 0]

    def f(oc, tt, psum):
        sl = oc % 2
        yt = sb["BF32"][:, sl * 2048 + tt * TT: sl * 2048 + (tt + 1) * TT]
        P.v.wait(S["pe"])
        if tt == 0:
            P.v.wait(S["st"], st_mark[sl])
        bcol = bias_fn(oc)
        if bcol is not None:
            P.v.tensor_scalar(out=yt, in0=psum, scalar1=bcol, scalar2=None,
                              op0=ALU.add).then_inc(S["dv"], 1)
        else:
            P.v.tensor_copy(out=yt, in_=psum).then_inc(S["dv"], 1)
        return (S["dv"], S["dv"].n)

    def flush(oc):
        sl = oc % 2
        if not hasattr(C, "accum_marks"):
            C.accum_marks = {}
        P.g.wait(S["dv"])
        P.g.dma_start(out=C.dram["xT"][oc * 128:(oc + 1) * 128, :],
                      in_=sb["BF32"][:, sl * 2048:sl * 2048 + 2048],
                      accum_op=ALU.add).then_inc(S["st"], 16)
        st_mark[sl] = S["st"].n
        C.accum_marks[oc] = S["st"].n
        P.g.wait(S["st"], S["st"].n)
        P.g.dma_start(out=C.scr_xn[:, oc * T:(oc + 1) * T],
                      in_=C.dram["xT"][oc * 128:(oc + 1) * 128, :]
                      ).then_inc(S["lds"], 16)
        C.xbf_marks[oc] = S["lds"].n
    f.flush = flush
    return f


def epi_final(C, bias_fn):
    P, sb, S = C.P, C.sb, C.sem
    ld_mark = [S["dv"].n]
    pl_mark = [S["pl"].n]
    st_mark = [0, 0]

    def f(oc, tt, psum):
        sl = oc % 2
        yt = sb["BF32"][:, sl * 2048 + tt * TT:sl * 2048 + (tt + 1) * TT]
        xrow = sb["BF32"][:, 4096 + tt * TT:4096 + (tt + 1) * TT]
        if tt == 0:
            P.y.wait(S["dv"], ld_mark[0])
            if pl_mark[0] > 0:
                P.y.wait(S["pl"], pl_mark[0])
                pl_mark[0] = 0
            P.y.dma_start(out=sb["BF32"][:, 4096:6144],
                          in_=C.dram["xT"][oc * 128:(oc + 1) * 128, :]
                          ).then_inc(S["ldx"], 16)
        P.v.wait(S["pe"])
        P.v.wait(S["ldx"])
        if tt == 0:
            P.v.wait(S["lds"], st_mark[sl])
        bcol = bias_fn(oc)
        if bcol is not None:
            P.v.tensor_scalar(out=yt, in0=psum, scalar1=bcol, scalar2=None,
                              op0=ALU.add).then_inc(S["dv"], 1)
        else:
            P.v.tensor_copy(out=yt, in_=psum).then_inc(S["dv"], 1)
        P.v.tensor_tensor(out=yt, in0=yt, in1=xrow, op=ALU.add).then_inc(S["dv"], 1)
        if tt == NTT - 1:
            ld_mark[0] = S["dv"].n
        return (S["dv"], S["dv"].n)

    def flush(oc):
        sl = oc % 2
        P.s.wait(S["dv"])
        P.s.dma_start(out=C.out[oc * 128:(oc + 1) * 128, :],
                      in_=sb["BF32"][:, sl * 2048:sl * 2048 + 2048]
                      ).then_inc(S["lds"], 16)
        st_mark[sl] = S["lds"].n
    f.flush = flush
    return f



def emit_xa(C, pre=None):
    P, sb, S, ps = C.P, C.sb, C.sem, C.ps
    C.xa_tt_dv = [None, None, None, None]

    def epi(oc, tt, psum):
        P.v.wait(S["pe"])
        P.v.tensor_scalar(out=sb["MED"][:, oc * T + tt * TT:oc * T + (tt + 1) * TT],
                          in0=psum, scalar1=sb["SF32"][:, 128 + oc:129 + oc],
                          scalar2=None, op0=ALU.mult).then_inc(S["dv"], 1)
        if oc == 1:
            C.xa_tt_dv[tt] = [(S["dv"], S["dv"].n)]
        return (S["dv"], S["dv"].n)

    proj_loop(C, "l0f_ablk", None, 2, 4096,
              terms_plain(C, lambda dc, t0, t1: xn_ap(C, dc, t0, t1)), epi,
              tt_gates=C.ln_tt_dv, pre=pre)


def emit_l0_ffn(C, final=False):
    pre = preload_w(C, "l0f_ablk", 4096, 2)
    ln_into_big(C, C.dram["xT"])
    emit_xa(C, pre=pre)
    terms = terms_lowrank(C, lambda dc, t0, t1: xn_ap(C, dc, t0, t1),
                          C.sb["MED"][:, 0:4096])
    bias_fn = lambda oc: C.sb["SF32"][:, oc:oc + 1]
    epi = epi_final(C, bias_fn) if final else epi_accum(C, bias_fn)
    proj_loop(C, "l0f_wblk", None, NCH, 4352, terms, epi, tt_gates=C.xa_tt_dv)


def emit_prep_w1(C):
    if getattr(C, "_w1_prepped", False):
        return
    C._w1_prepped = True
    P, sb, S, ps = C.P, C.sb, C.sem, C.ps
    P.y.wait(S["st"])
    P.y.wait(S["dv"])
    P.v.wait(S["st"])
    qi = sb["BF32"][:, 0:3072].bitcast(I32)
    Uh = [sb["BF32"][:, 3072:5120].bitcast(I32),
          sb["BF32"][:, 5120:7168].bitcast(I32)]
    wbfh = [sb["MED"][:, 0:2048], sb["MED"][:, 2048:4096]]
    stg = sb["MED"][:, 4096:8192]
    q_mark = 0
    wbf_rd = [S["pe"].n, S["pe"].n]
    stg_mark = 0
    bank_mark = [S["ac"].n, S["dv"].n]
    aff_dv = [0, 0]
    for nc_ in range(NCH):
        P.y.wait(S["dv"], q_mark)
        P.y.dma_start(out=qi, in_=C.dram["l1f_qblk"][nc_]).then_inc(S["ldx"], 16)
        P.v.wait(S["ldx"])
        for h in range(2):
            q3 = qi[:, 1536 * h:1536 * (h + 1)].rearrange("p (g c) -> p g c", c=3)
            U4 = Uh[h].rearrange("p (g j) -> p g j", j=4)
            P.v.tensor_scalar(out=U4[:, :, 0:1], in0=q3[:, :, 0:1], scalar1=63,
                              scalar2=None, op0=ALU.bitwise_and).then_inc(S["dv"], 1)
            P.v.tensor_scalar(out=U4[:, :, 1:2], in0=q3[:, :, 0:1], scalar1=6,
                              scalar2=63, op0=ALU.logical_shift_right,
                              op1=ALU.bitwise_and).then_inc(S["dv"], 1)
            P.v.tensor_scalar(out=U4[:, :, 2:3], in0=q3[:, :, 1:2], scalar1=63,
                              scalar2=None, op0=ALU.bitwise_and).then_inc(S["dv"], 1)
            P.v.tensor_scalar(out=U4[:, :, 3:4], in0=q3[:, :, 2:3], scalar1=63,
                              scalar2=None, op0=ALU.bitwise_and).then_inc(S["dv"], 1)
            if h == 1:
                q_mark = S["dv"].n
            P.v.wait(S["pe"], wbf_rd[h])
            P.v.tensor_scalar(out=wbfh[h], in0=Uh[h],
                              scalar1=sb["SF32"][:, 96 + nc_:97 + nc_],
                              scalar2=sb["SF32"][:, 64 + nc_:65 + nc_],
                              op0=ALU.subtract, op1=ALU.mult).then_inc(S["dv"], 1)
            aff_dv[h] = S["dv"].n
        for h in range(2):
            P.t.wait(S["dv"], aff_dv[h])
            for k in range(16):
                kk = 16 * h + k
                par = kk % 2
                pb = ps[6 + par]
                P.t.wait(S["ac"] if par == 0 else S["dv"], bank_mark[par])
                mm = P.t.transpose(pb[:, 0:128], wbfh[h][:, k * 128:(k + 1) * 128],
                                   C.ident)
                mm.then_inc(S["pe"], 1)
                eng = P.s if par == 0 else P.v
                eng.wait(S["pe"])
                if kk < 2:
                    eng.wait(S["lds"], stg_mark)
                if par == 0:
                    P.s.copy(out=stg[:, kk * 128:(kk + 1) * 128], in_=pb[:, 0:128]
                             ).then_inc(S["ac"], 1)
                    bank_mark[0] = S["ac"].n
                else:
                    P.v.tensor_copy(out=stg[:, kk * 128:(kk + 1) * 128],
                                    in_=pb[:, 0:128]).then_inc(S["dv"], 1)
                    bank_mark[1] = S["dv"].n
            wbf_rd[h] = S["pe"].n
        P.s.wait(S["dv"], bank_mark[1])
        P.s.dma_start(out=C.scr_w1[nc_], in_=stg).then_inc(S["lds"], 16)
        stg_mark = S["lds"].n
        C.med_lds = stg_mark


def emit_l1_ffn(C, final=False):
    emit_prep_w1(C)
    pre = preload_w(C, C.scr_w1, 4096, NCH, wq_engine="s")
    ln_into_big(C, C.dram["xT"])
    terms = terms_plain(C, lambda dc, t0, t1: xn_ap(C, dc, t0, t1))
    bias_fn = lambda oc: C.sb["SF32"][:, 32 + oc:33 + oc]
    epi = epi_final(C, bias_fn) if final else epi_accum(C, bias_fn)
    proj_loop(C, C.scr_w1, None, NCH, 4096, terms, epi, cast=False,
              wq_engine="s", tt_gates=C.ln_tt_dv, pre=pre)



def load_rope_tables(C, which):
    P, sb, S = C.P, C.sb, C.sem
    P.y.wait(S["dv"])
    P.y.wait(S["pe"])
    P.y.dma_start(out=sb["MED"][:, 0:2048], in_=C.dram[f"c_cos{which}"]
                  ).then_inc(S["ldx"], 16)
    P.y.dma_start(out=sb["MED"][:, 2048:4096], in_=C.dram[f"c_sin{which}"]
                  ).then_inc(S["ldx"], 16)


def epi_rope_store(C, scr_dst_fn):
    P, sb, S, ps = C.P, C.sb, C.sem, C.ps
    rot_mark = [S["dv"].n]
    asm_mark = [0]
    kb_mark = [S["pe"].n, S["pe"].n]

    def f(oc, tt, psum):
        sl = (oc * NTT + tt) % 2
        kb = sb["MED"][:, 4096 + sl * 512:4096 + (sl + 1) * 512]
        t_c = sb["MED"][:, 5120:5632]
        t_s = sb["MED"][:, 5632:6144]
        asm = sb["MED"][:, 6144 + tt * 512:6144 + (tt + 1) * 512]
        cos = sb["MED"][:, tt * 512:(tt + 1) * 512]
        sin = sb["MED"][:, 2048 + tt * 512:2048 + (tt + 1) * 512]
        P.s.wait(S["pe"])
        P.s.wait(S["pe"], kb_mark[sl])
        P.s.wait(S["dv"], rot_mark[0])
        P.s.activation(out=kb, in_=psum, func=AF.Copy).then_inc(S["ac"], 1)
        gate = (S["ac"], S["ac"].n)
        P.t.wait(S["ac"])
        P.t.wait(S["dv"], rot_mark[0])
        P.t.matmul(ps[4][:, :], C.prot, kb, start=True, stop=True
                   ).then_inc(S["pe"], 1)
        kb_mark[sl] = S["pe"].n
        P.v.wait(S["pe"])
        if tt == 0:
            P.v.wait(S["lds"], asm_mark[0])
        P.v.tensor_tensor(out=t_c, in0=kb, in1=cos, op=ALU.mult).then_inc(S["dv"], 1)
        P.v.tensor_tensor(out=t_s, in0=ps[4][:, :], in1=sin, op=ALU.mult
                          ).then_inc(S["dv"], 1)
        rot_mark[0] = S["dv"].n
        P.v.tensor_tensor(out=asm, in0=t_c, in1=t_s, op=ALU.add).then_inc(S["dv"], 1)
        return gate

    def flush(oc):
        P.s.wait(S["dv"])
        P.s.dma_start(out=scr_dst_fn(oc), in_=sb["MED"][:, 6144:8192]
                      ).then_inc(S["lds"], 16)
        asm_mark[0] = S["lds"].n
        C.med_lds = asm_mark[0]
    f.flush = flush
    return f


def emit_v_tm(C, vblk_name, nvq, scr_col0):
    P, sb, S, ps = C.P, C.sb, C.sem, C.ps
    if not hasattr(C, "wb_mark"):
        C.wb_mark = [0, 0]
    if getattr(C, "wb_wcols", None) != 4096:
        m = max(C.wb_mark)
        C.wb_mark = [m, m]
        C.wb_wcols = 4096
    bank_gate = {}
    g0 = (S["dv"], S["dv"].n)
    stg_mark = [0, 0]
    wv_of = {}

    def emit_load(vq):
        sl = vq % 2
        P.g.wait(S["pe"], C.wb_mark[sl])
        P.g.dma_start(out=sb["WB"][:, sl * 4096:(sl + 1) * 4096],
                      in_=C.dram[vblk_name][vq]).then_inc(S["ldw"], 16)
        wv_of[vq] = S["ldw"].n

    emit_load(0)
    if nvq > 1:
        emit_load(1)
    for vq in range(nvq):
        sl = vq % 2
        stg = sb["MED"][:, 4096 + sl * 2048:4096 + (sl + 1) * 2048]
        for tc in range(16):
            b = tc % 4
            if tc == 0:
                P.t.wait(S["ldw"], wv_of[vq])
            gate = bank_gate.get(b, g0)
            if gate[1] > 0:
                P.t.wait(*gate)
            for dc in range(NCH):
                mm = P.t.matmul(ps[b][:, 0:128],
                                xn_ap(C, dc, 0, T)[:, tc * 128:(tc + 1) * 128],
                                sb["WB"][:, sl * 4096 + dc * 128:sl * 4096 + (dc + 1) * 128],
                                start=(dc == 0), stop=(dc == NCH - 1))
            mm.then_inc(S["pe"], 1)
            if tc == 15:
                C.wb_mark[sl] = S["pe"].n
            P.v.wait(S["pe"])
            if tc == 0:
                P.v.wait(S["lds"], stg_mark[sl])
            P.v.tensor_copy(out=stg[:, tc * 128:(tc + 1) * 128], in_=ps[b][:, 0:128]
                            ).then_inc(S["dv"], 1)
            bank_gate[b] = (S["dv"], S["dv"].n)
        if vq + 2 < nvq:
            emit_load(vq + 2)
        P.s.wait(S["dv"])
        dst = C.scr_kv[:, scr_col0:scr_col0 + 16 * nvq * 128].rearrange(
            "p (tc v) -> p tc v", v=nvq * 128)[:, :, vq * 128:(vq + 1) * 128]
        P.s.dma_start(out=dst, in_=stg.rearrange("p (tc o) -> p tc o", o=128)
                      ).then_inc(S["lds"], 16)
        stg_mark[sl] = S["lds"].n
        C.med_lds = stg_mark[sl]


def emit_l1_attn_a(C, final=False):
    P, sb, S, ps = C.P, C.sb, C.sem, C.ps
    pre = preload_w(C, "l1a_kblk", 4096, 8)
    load_rope_tables(C, 1)
    ln_into_big(C, C.dram["xT"])
    proj_loop(C, "l1a_kblk", None, 8, 4096,
              terms_plain(C, lambda dc, t0, t1: xn_ap(C, dc, t0, t1)),
              epi_rope_store(C, lambda oc: C.scr_kv[:, oc * 2048:(oc + 1) * 2048]),
              tt_gates=C.ln_tt_dv, pre=pre)
    emit_v_tm(C, "l1a_vblk", 8, 16384)
    proj_loop(C, "l1a_qblk", None, NCH, 4096,
              terms_plain(C, lambda dc, t0, t1: xn_ap(C, dc, t0, t1)),
              epi_rope_store(C, lambda oc: C.scr_q[:, oc * 2048:(oc + 1) * 2048]))
    emit_attn_core(C, nheads=32, kv_of=lambda h: h // 4,
                   kr_cols=lambda kvh, kc: 0 + kvh * 2048 + kc * 128,
                   vt_cols=lambda kvh, kc: 16384 + kc * 1024 + kvh * 128,
                   scale=1.0 / np.sqrt(128.0), kind="softmax")
    _wo_pass(C, "l1a_oblk", final)


def _wo_pass(C, oblk, final, bias_fn=None):
    P, sb, S = C.P, C.sb, C.sem
    P.y.wait(S["lds"])
    P.y.wait(S["pe"])
    P.y.wait(S["dv"])
    P.y.dma_start(out=sb["BIG"][:, 0:65536], in_=C.scr_attn).then_inc(S["ldx"], 16)
    ldx_big = S["ldx"].n
    terms = terms_plain(C, lambda dc, t0, t1: xn_ap(C, dc, t0, t1))
    if bias_fn is None:
        bias_fn = lambda oc: None
    epi = epi_final(C, bias_fn) if final else epi_accum(C, bias_fn)
    proj_loop(C, oblk, None, NCH, 4096, terms, epi,
              first_waits=((C.sem["ldx"], ldx_big),))


def emit_attn_core(C, nheads, kv_of, kr_cols, vt_cols, scale, kind,
                   alibi=None, kv_cols=(16384, 16384)):
    P, sb, S, ps = C.P, C.sb, C.sem, C.ps
    big = sb["BIG"]
    P.y.wait(S["lds"])
    P.y.wait(S["pe"])
    P.y.wait(S["dv"])
    P.y.wait(S["st"])
    P.y.dma_start(out=big[:, 0:kv_cols[0]], in_=C.scr_kv[:, 0:kv_cols[0]]
                  ).then_inc(S["ldx"], 16)
    P.y.dma_start(out=big[:, 16384:16384 + kv_cols[1]],
                  in_=C.scr_kv[:, 16384:16384 + kv_cols[1]]).then_inc(S["ldx"], 16)
    et_of = [big[:, 4096:7040], big[:, 9984:12928]]
    maskz = big[:, 7040:9984]
    PT = [40960, 49152]
    et_mark = [0, 0]
    et_ready = [0, 0]
    if alibi is not None:
        P.y.dma_start(out=maskz, in_=C.dram["c_bmaskz"]).then_inc(S["ldx"], 16)
        P.y.dma_start(out=sb["BF32"][:, 0:2944], in_=C.dram["c_balibi"]
                      ).then_inc(S["ldx"], 16)
    ldx_kv = S["ldx"].n
    qh_mark = [0, 0]
    astg_mark = [0, 0]
    probs_mark = [[0] * 16, [0] * 16]
    bank_gate = {}
    g0 = (S["dv"], S["dv"].n)
    sums_mark = [0]
    av_mark = [0]
    b5_gate = [0]
    rb_rd = [0, 0]
    pe_sums = [0, 0]

    def pt_tile(sub, kc, w=512):
        return big[:, PT[sub] + kc * 512:PT[sub] + kc * 512 + w]

    def emit_et(h):
        sub = h % 2
        et = et_of[sub]
        P.s.wait(S["ldx"], ldx_kv)
        P.s.wait(S["pl"], et_mark[sub])
        P.s.activation(out=et, in_=sb["BF32"][:, 0:2944], func=AF.Exp,
                       scale=np.float32(alibi[h])).then_inc(S["ac"], 1)
        P.g.wait(S["ac"], S["ac"].n)
        P.g.tensor_tensor(out=et, in0=et, in1=maskz, op=ALU.mult
                          ).then_inc(S["pl"], 1)
        et_ready[sub] = S["pl"].n

    def s1(h, qt, qv):
        sub = h % 2
        kvh = kv_of(h)
        et = et_of[sub]
        qh = big[:, 32768 + sub * 2048:32768 + (sub + 1) * 2048]
        nkc = 4 * (qt + 1)
        for kc in range(nkc):
            b = kc % 4
            gate = bank_gate.get(b, g0)
            if gate[1] > 0:
                P.t.wait(*gate)
            if h < 2 and qt == 0 and kc == 0:
                P.t.wait(S["ldx"], ldx_kv)
            if qt == 0 and kc == 0:
                P.t.wait(S["ldx"], qv)
            P.t.matmul(ps[b][:, :],
                       big[:, kr_cols(kvh, kc):kr_cols(kvh, kc) + 128],
                       qh[:, qt * 512:(qt + 1) * 512],
                       start=True, stop=True).then_inc(S["pe"], 1)
            P.s.wait(S["pe"])
            P.s.wait(S["pe"], probs_mark[sub][kc])
            P.s.activation(out=pt_tile(sub, kc), in_=ps[b][:, :],
                           func=AF.Exp, scale=np.float32(scale)
                           ).then_inc(S["ac"], 1)
            bank_gate[b] = (S["ac"], S["ac"].n)
            if alibi is not None:
                off = (qt * 512 - kc * 128) + 384
                eng, sem = (P.g, "pl") if kc % 4 == 3 else (P.v, "dv")
                eng.wait(S["ac"], S["ac"].n)
                eng.wait(S["pl"], et_ready[sub])
                eng.tensor_tensor(out=pt_tile(sub, kc), in0=pt_tile(sub, kc),
                                  in1=et[:, off:off + 512],
                                  op=ALU.mult).then_inc(S[sem], 1)
            elif kc * 128 >= qt * 512:
                off = (qt * 512 - kc * 128) + 384
                eng, sem = (P.g, "pl") if kc % 2 == 1 else (P.v, "dv")
                eng.wait(S["ac"], S["ac"].n)
                eng.tensor_tensor(out=pt_tile(sub, kc), in0=pt_tile(sub, kc),
                                  in1=C.mask896[:, off:off + 512],
                                  op=ALU.mult).then_inc(S[sem], 1)
        return (S["ac"].n, S["pl"].n, S["dv"].n)

    def s2(h, qt, marks):
        sub = h % 2
        kvh = kv_of(h)
        nkc = 4 * (qt + 1)
        acm, plm, dvm = marks
        P.t.wait(S["ac"], acm)
        if plm > 0:
            P.t.wait(S["pl"], plm)
        P.t.wait(S["dv"], dvm)
        P.t.wait(S["dv"], sums_mark[0])
        if b5_gate[0] > 0:
            P.t.wait(S["ac"], b5_gate[0])
        for kc in range(nkc):
            mm = P.t.matmul(ps[5][0:1, :], C.ones128, pt_tile(sub, kc),
                            start=(kc == 0), stop=(kc == nkc - 1))
        mm.then_inc(S["pe"], 1)
        pe_sums[sub] = S["pe"].n
        P.t.wait(S["dv"], av_mark[0])
        for kc in range(nkc):
            mm = P.t.matmul(ps[4][:, :],
                            big[:, vt_cols(kvh, kc):vt_cols(kvh, kc) + 128],
                            pt_tile(sub, kc),
                            start=(kc == 0), stop=(kc == nkc - 1))
        mm.then_inc(S["pe"], 1)
        pe_av = S["pe"].n
        for kc in range(nkc):
            probs_mark[sub][kc] = pe_av
        r_sum = sb["ROWS"][:, 512 + sub * 512:1024 + sub * 512]
        P.v.wait(S["pe"], pe_sums[sub])
        P.v.reciprocal(out=r_sum, in_=ps[5][0:1, :]).then_inc(S["dv"], 1)
        sums_mark[0] = S["dv"].n

    def s3(h, qt):
        sub = h % 2
        astg = big[:, 36864 + sub * 2048:36864 + (sub + 1) * 2048]
        r_sum = sb["ROWS"][:, 512 + sub * 512:1024 + sub * 512]
        rb = sb["BF32"][:, 4096 + sub * 512:4608 + sub * 512]
        P.t.wait(S["dv"], sums_mark[0])
        P.t.matmul(ps[5][:, :], C.onesrow_f[:, 0:128], r_sum,
                   start=True, stop=True).then_inc(S["pe"], 1)
        P.s.wait(S["pe"])
        P.s.wait(S["dv"], rb_rd[sub])
        P.s.copy(out=rb, in_=ps[5][:, :]).then_inc(S["ac"], 1)
        b5_gate[0] = S["ac"].n
        P.v.wait(S["ac"])
        if qt == 0:
            P.v.wait(S["lds"], astg_mark[sub])
        P.v.tensor_tensor(out=astg[:, qt * 512:(qt + 1) * 512],
                          in0=ps[4][:, :], in1=rb, op=ALU.mult
                          ).then_inc(S["dv"], 1)
        av_mark[0] = S["dv"].n
        rb_rd[sub] = S["dv"].n

    for hp in range(nheads // 2):
        qv = [0, 0]
        for sub in range(2):
            h = 2 * hp + sub
            qh = big[:, 32768 + sub * 2048:32768 + (sub + 1) * 2048]
            P.y.wait(S["pe"], qh_mark[sub])
            P.y.dma_start(out=qh, in_=C.scr_q[:, h * 2048:(h + 1) * 2048]
                          ).then_inc(S["ldx"], 16)
            qv[sub] = S["ldx"].n
            if alibi is not None:
                emit_et(h)
        for qt in range(NTT):
            mA = s1(2 * hp, qt, qv[0])
            mB = s1(2 * hp + 1, qt, qv[1])
            s2(2 * hp, qt, mA)
            s3(2 * hp, qt)
            s2(2 * hp + 1, qt, mB)
            s3(2 * hp + 1, qt)
        for sub in range(2):
            h = 2 * hp + sub
            astg = big[:, 36864 + sub * 2048:36864 + (sub + 1) * 2048]
            P.s.wait(S["dv"])
            P.s.dma_start(out=C.scr_attn[:, h * 2048:(h + 1) * 2048], in_=astg
                          ).then_inc(S["lds"], 16)
            astg_mark[sub] = S["lds"].n
        qh_mark[0] = S["pe"].n
        qh_mark[1] = S["pe"].n
        et_mark[0] = S["pl"].n
        et_mark[1] = S["pl"].n


def epi_store(C, scr_dst_fn):
    P, sb, S = C.P, C.sb, C.sem
    asm_mark = [0]

    def f(oc, tt, psum):
        asm = sb["MED"][:, 6144 + tt * 512:6144 + (tt + 1) * 512]
        P.s.wait(S["pe"])
        if tt == 0:
            P.s.wait(S["lds"], asm_mark[0])
        P.s.activation(out=asm, in_=psum, func=AF.Copy).then_inc(S["ac"], 1)
        return (S["ac"], S["ac"].n)

    def flush(oc):
        P.s.dma_start(out=scr_dst_fn(oc), in_=sb["MED"][:, 6144:8192]
                      ).then_inc(S["lds"], 16)
        asm_mark[0] = S["lds"].n
        C.med_lds = asm_mark[0]
    f.flush = flush
    return f


def alibi_slopes(n=32):
    import math
    cp2 = 2 ** int(math.floor(math.log2(n)))
    base = 2.0 ** (-(2.0 ** (-(math.log2(cp2) - 3))))
    slopes = [base ** i for i in range(1, cp2 + 1)]
    if cp2 != n:
        extra_base = 2.0 ** (-(2.0 ** (-(math.log2(2 * cp2) - 3))))
        rem = min(cp2, n - cp2)
        slopes += [extra_base ** i for i in range(1, 1 + 2 * rem, 2)]
    return slopes


def emit_l1_attn_b(C, final=False):
    P, sb, S = C.P, C.sb, C.sem
    pre = preload_w(C, "l1b_kblk", 4096, 1)
    ln_into_big(C, C.dram["xT"])
    proj_loop(C, "l1b_kblk", None, 1, 4096,
              terms_plain(C, lambda dc, t0, t1: xn_ap(C, dc, t0, t1)),
              epi_store(C, lambda oc: C.scr_kv[:, oc * 2048:(oc + 1) * 2048]),
              tt_gates=C.ln_tt_dv, pre=pre)
    emit_v_tm(C, "l1b_vblk", 1, 16384)
    proj_loop(C, "l1b_qblk", None, NCH, 4096,
              terms_plain(C, lambda dc, t0, t1: xn_ap(C, dc, t0, t1)),
              epi_store(C, lambda oc: C.scr_q[:, oc * 2048:(oc + 1) * 2048]))
    emit_attn_core(C, nheads=32, kv_of=lambda h: 0,
                   kr_cols=lambda kvh, kc: kc * 128,
                   vt_cols=lambda kvh, kc: 16384 + kc * 128,
                   scale=1.0 / np.sqrt(128.0), kind="softmax",
                   alibi=alibi_slopes(32), kv_cols=(2048, 2048))
    _wo_pass(C, "l1b_oblk", final)


def ln_stats_only(C, x_src):
    P, sb, S, ps = C.P, C.sb, C.sem, C.ps
    xr = x_src.rearrange("(c p) t -> p c t", p=128)
    P.y.wait(S["st"])
    dv_mark = [S["dv"].n, S["dv"].n]
    pe_mark = [S["pe"].n, S["pe"].n]
    pe_entry = S["pe"].n
    for tt in range(NTT):
        P.t.wait(S["dv"])
        for cb in range(16):
            sl = cb % 2
            buf = sb["BF32"][:, 4096 + sl * 1024:4096 + (sl + 1) * 1024]
            xb = sb["MED"][:, sl * 1024:(sl + 1) * 1024]
            x2 = sb["MED"][:, 2048 + sl * 1024:2048 + (sl + 1) * 1024]
            P.y.wait(S["dv"], dv_mark[sl])
            P.y.dma_start(out=buf,
                          in_=xr[:, 2 * cb:2 * cb + 2, tt * TT:(tt + 1) * TT]
                          ).then_inc(S["ldx"], 16)
            P.v.wait(S["ldx"])
            P.v.wait(S["pe"], pe_mark[sl])
            P.v.tensor_copy(out=xb, in_=buf).then_inc(S["dv"], 1)
            P.s.wait(S["dv"])
            P.s.activation(out=x2, in_=xb, func=AF.Square).then_inc(S["ac"], 1)
            P.t.wait(S["dv"])
            first = (cb == 0)
            for k in range(2):
                P.t.matmul(ps[4][0:1, :], C.ones128, xb[:, k * TT:(k + 1) * TT],
                           start=(first and k == 0), stop=(cb == 15 and k == 1))
            P.t.wait(S["ac"])
            for k in range(2):
                mm = P.t.matmul(ps[5][0:1, :], C.ones128, x2[:, k * TT:(k + 1) * TT],
                                start=(first and k == 0), stop=(cb == 15 and k == 1))
            mm.then_inc(S["pe"], 1)
            pe_mark[sl] = S["pe"].n
            dv_mark[sl] = S["dv"].n
        r_mean = sb["ROWS"][:, 512:1024]
        r_var = sb["ROWS"][:, 1024:1536]
        r_m2 = sb["ROWS"][:, 1536:2048]
        r_rstd = sb["ROWS"][:, 2048:2560]
        r_mr = sb["ROWS"][:, 2560:3072]
        P.v.wait(S["pe"])
        P.v.tensor_scalar(out=r_mean, in0=ps[4][0:1, :], scalar1=1.0 / D,
                          scalar2=None, op0=ALU.mult).then_inc(S["dv"], 1)
        P.v.tensor_scalar(out=r_var, in0=ps[5][0:1, :], scalar1=1.0 / D,
                          scalar2=None, op0=ALU.mult).then_inc(S["dv"], 1)
        P.s.wait(S["dv"])
        P.s.activation(out=r_m2, in_=r_mean, func=AF.Square).then_inc(S["ac"], 1)
        P.v.wait(S["ac"])
        P.v.tensor_tensor(out=r_var, in0=r_var, in1=r_m2, op=ALU.subtract
                          ).then_inc(S["dv"], 1)
        P.s.wait(S["dv"])
        P.s.activation(out=r_var, in_=r_var, func=AF.Ln, bias=np.float32(1e-5)
                       ).then_inc(S["ac"], 1)
        P.s.activation(out=r_rstd, in_=r_var, func=AF.Exp, scale=np.float32(-0.5)
                       ).then_inc(S["ac"], 1)
        P.v.wait(S["ac"])
        P.v.tensor_tensor(out=r_mr, in0=r_mean, in1=r_rstd, op=ALU.mult
                          ).then_inc(S["dv"], 1)
        P.t.wait(S["dv"])
        P.t.matmul(ps[4][:, :], C.onesrow_f[:, 0:128], r_rstd, start=True, stop=True)
        mm = P.t.matmul(ps[5][:, :], C.onesrow_f[:, 0:128], r_mr, start=True, stop=True)
        mm.then_inc(S["pe"], 1)
        P.v.wait(S["pe"])
        if tt == 0:
            P.v.wait(S["pe"], pe_entry)
            P.v.wait(S["lds"])
        P.v.tensor_copy(out=sb["BF32"][:, tt * TT:(tt + 1) * TT], in_=ps[4][:, :]
                        ).then_inc(S["dv"], 1)
        P.v.tensor_copy(out=sb["BF32"][:, 2048 + tt * TT:2048 + (tt + 1) * TT],
                        in_=ps[5][:, :]).then_inc(S["dv"], 1)


def emit_bind_prep(C):
    P, sb, S, ps = C.P, C.sb, C.sem, C.ps
    B32 = sb["BF32"]
    P.y.wait(S["dv"])
    P.y.wait(S["pe"])
    P.y.wait(S["ac"])
    for i, nm in enumerate(["l0a_bvq", "l0a_bvk", "l0a_bvv"]):
        src_ap = C.dram[nm].flatten().rearrange("(c p) -> p c", p=128)
        P.y.dma_start(out=B32[:, i * 32:(i + 1) * 32], in_=src_ap,
                      allow_slow_non_contiguous=True).then_inc(S["ldx"], 16)
    P.y.dma_start(out=B32[0:96, 3168:3264], in_=C.dram["c_ghead"]
                  ).then_inc(S["ldx"], 16)
    P.v.wait(S["ldx"])
    P.v.memset(B32[:, 96:512], 0.0).then_inc(S["dv"], 1)
    bvs = B32[:, 0:512]
    sgn = B32[:, 512:1024]
    s2q = B32[:, 1024:1536]
    ab = B32[:, 1536:2048]
    colsb = B32[0:96, 2048:2176]
    alb_sb = B32[:, 2176:2272]
    s2b = B32[:, 2560:3072]
    G = B32[:, 3072:3168]
    abb = sb["MED"][:, 0:512]
    r1 = sb["ROWS"][:, 512:608]
    P.s.wait(S["ldx"])
    P.s.wait(S["dv"])
    P.s.activation(out=sgn, in_=bvs, func=AF.Sign).then_inc(S["ac"], 1)
    P.s.activation(out=s2q, in_=sgn, func=AF.Square).then_inc(S["ac"], 1)
    P.v.wait(S["ac"])
    P.v.tensor_tensor(out=ab, in0=bvs, in1=sgn, op=ALU.mult).then_inc(S["dv"], 1)
    P.v.tensor_copy(out=abb, in_=ab).then_inc(S["dv"], 1)
    P.t.wait(S["dv"])
    P.t.matmul(ps[4][0:1, 0:96], C.ones128, abb[:, 0:96], start=True, stop=True
               ).then_inc(S["pe"], 1)
    P.v.wait(S["pe"])
    P.v.tensor_copy(out=r1, in_=ps[4][0:1, 0:96]).then_inc(S["dv"], 1)
    P.s.wait(S["dv"])
    P.s.dma_start(out=C.scr_misc[0:1, 8192:8288], in_=r1).then_inc(S["lds"], 16)
    P.y.wait(S["lds"])
    P.y.dma_start(out=B32[0:96, 2048:2049],
                  in_=C.scr_misc[0:1, 8192:8288].rearrange("o (p c) -> (o p) c", c=1)
                  ).then_inc(S["ldx"], 16)
    P.v.wait(S["ldx"])
    P.v.tensor_copy(out=colsb, in_=B32[0:96, 2048:2049].broadcast_to((96, 128))
                    ).then_inc(S["dv"], 1)
    P.t.wait(S["dv"])
    P.t.matmul(ps[5][:, 0:96], colsb, B32[0:96, 3168:3264], start=True, stop=True
               ).then_inc(S["pe"], 1)
    P.s.wait(S["pe"])
    P.s.copy(out=alb_sb, in_=ps[5][:, 0:96]).then_inc(S["ac"], 1)
    P.v.tensor_tensor(out=s2b, in0=sgn, in1=s2q, op=ALU.subtract).then_inc(S["dv"], 1)
    P.s.wait(S["dv"])
    P.s.add(out=s2b, in_=s2b, add=1.0).then_inc(S["ac"], 1)
    P.v.wait(S["ac"])
    P.v.tensor_tensor(out=G, in0=s2b[:, 0:96], in1=alb_sb, op=ALU.mult
                      ).then_inc(S["dv"], 1)
    P.v.tensor_copy(out=B32[:, 3584:4096], in_=B32[:, 1024:1536]
                    ).then_inc(S["dv"], 1)
    P.v.tensor_tensor(out=sb["SF32"][:, 388:420], in0=G[:, 0:32], in1=G[:, 32:64],
                      op=ALU.mult).then_inc(S["dv"], 1)
    P.v.tensor_copy(out=B32[:, 3584:4096], in_=B32[:, 1024:1536]
                    ).then_inc(S["dv"], 1)
    P.v.tensor_copy(out=sb["SF32"][:, 420:452], in_=G[:, 64:96]).then_inc(S["dv"], 1)
    if getattr(C, "dump", None) == "scr_misc":
        P.s.wait(S["dv"])
        P.s.dma_start(out=C.scr_misc[0:1, 0:4096].rearrange("o (c p) -> (o p) c", p=128),
                      in_=sb["SF32"][:, 388:420],
                      allow_slow_non_contiguous=True).then_inc(S["lds"], 16)
        P.s.dma_start(out=C.scr_misc[0:1, 4096:8192].rearrange("o (c p) -> (o p) c", p=128),
                      in_=sb["SF32"][:, 420:452],
                      allow_slow_non_contiguous=True).then_inc(S["lds"], 16)


def emit_l0_attn_a(C, final=False):
    P, sb, S, ps = C.P, C.sb, C.sem, C.ps
    big = sb["BIG"]
    med = sb["MED"]
    emit_bind_prep(C)
    ln_stats_only(C, C.dram["xT"])
    rb = sb["BF32"][:, 0:2048]
    mrb = sb["BF32"][:, 2048:4096]
    xr = C.dram["xT"].rearrange("(c p) t -> p c t", p=128)
    probs_mark = [0] * 16
    bank_gate = {}
    g0 = (S["dv"], S["dv"].n)
    tr_mark = [S["ac"].n, S["ac"].n]
    av_mark = [S["dv"].n, S["dv"].n]
    aout_mark = [0]
    scores_end = [0]
    xn_pe_free = [0]
    xld_of = {}

    def emit_load_x(h):
        bx = 16384 * (h % 2)
        xbig = big[:, bx:bx + 16384].rearrange("p (c t) -> p c t", t=T)
        P.g.wait(S["pe"], scores_end[0])
        if h == 0:
            P.g.wait(S["dv"])
        xld = [0, 0]
        for hf in range(2):
            P.g.dma_start(out=xbig[:, 4 * hf:4 * hf + 4, :],
                          in_=xr[:, 8 * h + 4 * hf:8 * h + 4 * hf + 4, :]
                          ).then_inc(S["ldx"], 16)
            xld[hf] = S["ldx"].n
        xld_of[h] = xld

    emit_load_x(0)
    for h in range(4):
        bx = 16384 * (h % 2)
        bk = 16384 - bx
        xld = xld_of[h]
        for c8 in range(8):
            dc = 8 * h + c8
            t1 = sb["BF32"][:, 4096:6144]
            xs = big[:, bx + c8 * 2048:bx + (c8 + 1) * 2048]
            P.v.wait(S["ldx"], xld[c8 // 4])
            P.v.wait(S["pe"], xn_pe_free[0])
            P.v.tensor_tensor(out=t1, in0=xs, in1=rb, op=ALU.mult
                              ).then_inc(S["dv"], 1)
            P.v.tensor_tensor(out=xs, in0=t1, in1=mrb, op=ALU.subtract
                              ).then_inc(S["dv"], 1)
            P.v.tensor_scalar(out=big[:, bk + c8 * 2048:bk + (c8 + 1) * 2048],
                              in0=xs,
                              scalar1=sb["SF32"][:, 388 + dc:389 + dc],
                              scalar2=None, op0=ALU.mult).then_inc(S["dv"], 1)
        prep_dv = S["dv"].n
        for c8 in range(8):
            P.t.wait(S["dv"], prep_dv)
            for tc in range(16):
                pb = ps[6 + (tc % 2)]
                P.t.wait(S["ac"], tr_mark[tc % 2])
                P.t.transpose(pb[:, 0:128],
                              big[:, bx + c8 * 2048 + tc * 128:
                                   bx + c8 * 2048 + (tc + 1) * 128],
                              C.ident).then_inc(S["pe"], 1)
                P.s.wait(S["pe"])
                P.s.copy(out=big[:, 32768 + tc * 1024 + c8 * 128:
                                 32768 + tc * 1024 + (c8 + 1) * 128],
                         in_=pb[:, 0:128]).then_inc(S["ac"], 1)
                tr_mark[tc % 2] = S["ac"].n
        for qt in range(NTT):
            nkc = 4 * (qt + 1)
            for kc in range(nkc):
                b = kc % 4
                gate = bank_gate.get(b, g0)
                if gate[1] > 0:
                    P.t.wait(*gate)
                if kc == 0:
                    P.t.wait(S["dv"], prep_dv)
                for c8 in range(8):
                    mm = P.t.matmul(ps[b][:, :],
                                    big[:, bk + c8 * 2048 + kc * 128:
                                         bk + c8 * 2048 + (kc + 1) * 128],
                                    big[:, bx + c8 * 2048 + qt * 512:
                                         bx + c8 * 2048 + (qt + 1) * 512],
                                    start=(c8 == 0), stop=(c8 == 7))
                mm.then_inc(S["pe"], 1)
                P.s.wait(S["pe"])
                P.s.wait(S["pe"], probs_mark[kc])
                P.s.activation(out=med[:, kc * 512:(kc + 1) * 512], in_=ps[b][:, :],
                               func=AF.Sigmoid, scale=np.float32(0.125)
                               ).then_inc(S["ac"], 1)
                bank_gate[b] = (S["ac"], S["ac"].n)
                if kc * 128 >= qt * 512:
                    off = (qt * 512 - kc * 128) + 384
                    P.v.wait(S["ac"])
                    P.v.tensor_tensor(out=med[:, kc * 512:(kc + 1) * 512],
                                      in0=med[:, kc * 512:(kc + 1) * 512],
                                      in1=C.mask896[:, off:off + 512],
                                      op=ALU.mult).then_inc(S["dv"], 1)
            if qt == NTT - 1:
                scores_end[0] = S["pe"].n
            P.t.wait(S["ac"])
            P.t.wait(S["dv"])
            for c8 in range(8):
                bb = 4 + (c8 % 2)
                P.t.wait(S["dv"], av_mark[c8 % 2])
                for kc in range(nkc):
                    mm = P.t.matmul(ps[bb][:, :],
                                    big[:, 32768 + kc * 1024 + c8 * 128:
                                         32768 + kc * 1024 + (c8 + 1) * 128],
                                    med[:, kc * 512:(kc + 1) * 512],
                                    start=(kc == 0), stop=(kc == nkc - 1))
                mm.then_inc(S["pe"], 1)
                dco = 8 * h + c8
                P.v.wait(S["pe"])
                if qt == 0 and c8 == 0:
                    P.v.wait(S["st"], aout_mark[0])
                P.v.tensor_scalar(out=big[:, 49152 + c8 * 2048 + qt * 512:
                                          49152 + c8 * 2048 + (qt + 1) * 512],
                                  in0=ps[bb][:, :],
                                  scalar1=sb["SF32"][:, 420 + dco:421 + dco],
                                  scalar2=None, op0=ALU.mult).then_inc(S["dv"], 1)
                av_mark[c8 % 2] = S["dv"].n
            for kc in range(nkc):
                probs_mark[kc] = S["pe"].n
        xn_pe_free[0] = scores_end[0]
        if h + 1 < 4:
            emit_load_x(h + 1)
        P.g.wait(S["dv"])
        for c8 in range(8):
            dc = 8 * h + c8
            P.g.dma_start(out=C.dram["xT"][dc * 128:(dc + 1) * 128, :],
                          in_=big[:, 49152 + c8 * 2048:49152 + (c8 + 1) * 2048],
                          accum_op=ALU.add).then_inc(S["st"], 16)
        aout_mark[0] = S["st"].n
        P.g.wait(S["st"], S["st"].n)
        P.g.dma_start(out=C.scr_xn[:, 8 * h * T:(8 * h + 8) * T].rearrange(
                          "p (c t) -> p c t", t=T),
                      in_=xr[:, 8 * h:8 * h + 8, :]).then_inc(S["lds"], 16)
        for c8 in range(8):
            C.xbf_marks[8 * h + c8] = S["lds"].n
    if final:
        P.y.wait(S["st"])
        P.y.dma_start(out=C.out, in_=C.dram["xT"]).then_inc(S["ldx"], 16)


def emit_bind_only(C, final=False):
    emit_bind_prep(C)
    if final:
        P, S = C.P, C.sem
        P.y.wait(S["lds"])
        P.y.dma_start(out=C.out, in_=C.dram["xT"]).then_inc(S["ldx"], 16)


def emit_l0b_windows(C):
    P, sb, S, ps = C.P, C.sb, C.sem, C.ps
    big = sb["BIG"]
    med = sb["MED"]
    P.y.wait(S["lds"])
    P.y.wait(S["pe"])
    P.y.wait(S["dv"])
    P.y.dma_start(out=big[:, 0:2048], in_=C.scr_kv[:, 0:2048]).then_inc(S["ldx"], 16)
    P.y.dma_start(out=big[:, 16384:18432], in_=C.scr_kv[:, 16384:18432]
                  ).then_inc(S["ldx"], 16)
    P.y.dma_start(out=sb["ROWS"][:, 512:1024], in_=C.dram["c_gmid"]).then_inc(S["ldx"], 16)
    P.y.dma_start(out=sb["ROWS"][:, 1024:1536], in_=C.dram["c_gw0"]).then_inc(S["ldx"], 16)
    P.y.dma_start(out=sb["ROWS"][:, 1536:1792], in_=C.dram["c_gw7"]).then_inc(S["ldx"], 16)
    ldx_kv = S["ldx"].n
    gb = {"mid": sb["BF32"][:, 0:512], "w0": sb["BF32"][:, 512:1024],
          "w7": sb["BF32"][:, 1024:1280]}
    P.t.wait(S["ldx"], ldx_kv)
    P.t.wait(S["dv"])
    P.t.wait(S["ac"])
    P.t.matmul(ps[3][:, :], C.onesrow_f[:, 0:128], sb["ROWS"][:, 512:1024],
               start=True, stop=True).then_inc(S["pe"], 1)
    P.s.wait(S["pe"])
    P.s.copy(out=gb["mid"], in_=ps[3][:, :]).then_inc(S["ac"], 1)
    P.t.wait(S["ac"])
    P.t.matmul(ps[3][:, :], C.onesrow_f[:, 0:128], sb["ROWS"][:, 1024:1536],
               start=True, stop=True).then_inc(S["pe"], 1)
    P.s.wait(S["pe"])
    P.s.copy(out=gb["w0"], in_=ps[3][:, :]).then_inc(S["ac"], 1)
    P.t.wait(S["ac"])
    P.t.matmul(ps[3][:, 0:256], C.onesrow_f[:, 0:128], sb["ROWS"][:, 1536:1792],
               start=True, stop=True).then_inc(S["pe"], 1)
    P.s.wait(S["pe"])
    P.s.copy(out=gb["w7"], in_=ps[3][:, 0:256]).then_inc(S["ac"], 1)
    qh_mark = [0, 0, 0, 0]
    astg_mark = [0, 0]
    probs_mark = [[0] * 4, [0] * 4]
    bank_gate = [0, 0]
    b5_gate = [0, 0]
    sums_mark = [0, 0]
    av_mark = [0, 0]
    rb_rd = [0, 0]
    dst_pl = [0, 0]
    qv = [0, 0]
    pe_sums = [0, 0]
    sc = [0]

    def qh_of(h):
        return big[:, 32768 + (h % 4) * 2048:32768 + (h % 4 + 1) * 2048]

    def s1(h, w):
        sub = h % 2
        qh = qh_of(h)
        q0 = 256 * w
        qw = 512 if w < 7 else 256
        for kc in range(qw // 128):
            b = sc[0] % 2
            sc[0] += 1
            if bank_gate[b] > 0:
                P.t.wait(S["ac"], bank_gate[b])
            if w == 0 and kc == 0:
                P.t.wait(S["ldx"], qv[sub])
            P.t.matmul(ps[b][:, 0:qw],
                       big[:, q0 + kc * 128:q0 + (kc + 1) * 128],
                       qh[:, q0:q0 + qw],
                       start=True, stop=True).then_inc(S["pe"], 1)
            P.s.wait(S["pe"])
            P.s.wait(S["pe"], probs_mark[sub][kc])
            P.s.activation(out=med[:, sub * 2048 + kc * 512:sub * 2048 + kc * 512 + qw],
                           in_=ps[b][:, 0:qw],
                           func=AF.Sigmoid, scale=np.float32(1.0 / np.sqrt(128.0))
                           ).then_inc(S["ac"], 1)
            bank_gate[b] = S["ac"].n
        return S["ac"].n

    def s2(h, w, sig_ac):
        sub = h % 2
        ps_s = ps[5] if sub == 0 else ps[2]
        ps_av = ps[4] if sub == 0 else ps[3]
        q0 = 256 * w
        qw = 512 if w < 7 else 256
        nkc = qw // 128
        P.t.wait(S["ac"], sig_ac)
        if b5_gate[sub] > 0:
            P.t.wait(S["ac"], b5_gate[sub])
        for kc in range(nkc):
            mm = P.t.matmul(ps_s[0:1, 0:qw], C.ones128,
                            med[:, sub * 2048 + kc * 512:sub * 2048 + kc * 512 + qw],
                            start=(kc == 0), stop=(kc == nkc - 1))
        mm.then_inc(S["pe"], 1)
        pe_sums[sub] = S["pe"].n
        if av_mark[sub] > 0:
            P.t.wait(S["dv"], av_mark[sub])
        for kc in range(nkc):
            mm = P.t.matmul(ps_av[:, 0:qw],
                            big[:, 16384 + (2 * w + kc) * 128:16384 + (2 * w + kc + 1) * 128],
                            med[:, sub * 2048 + kc * 512:sub * 2048 + kc * 512 + qw],
                            start=(kc == 0), stop=(kc == nkc - 1))
        mm.then_inc(S["pe"], 1)
        pe_av = S["pe"].n
        for kc in range(nkc):
            probs_mark[sub][kc] = pe_av
        r_sum = sb["ROWS"][:, 1792 + sub * 512:1792 + sub * 512 + qw]
        P.v.wait(S["pe"], pe_sums[sub])
        P.v.reciprocal(out=r_sum, in_=ps_s[0:1, 0:qw]).then_inc(S["dv"], 1)
        sums_mark[sub] = S["dv"].n

    def s3(h, w):
        sub = h % 2
        ps_s = ps[5] if sub == 0 else ps[2]
        ps_av = ps[4] if sub == 0 else ps[3]
        q0 = 256 * w
        qw = 512 if w < 7 else 256
        prof = gb["w0"] if w == 0 else (gb["w7"] if w == 7 else gb["mid"])
        r_sum = sb["ROWS"][:, 1792 + sub * 512:1792 + sub * 512 + qw]
        rb_sb = sb["BF32"][:, 1536 + sub * 512:1536 + sub * 512 + qw]
        t_sb = sb["BF32"][:, 2560 + sub * 512:2560 + sub * 512 + qw]
        E = big[:, 40960 + sub * 4096:40960 + sub * 4096 + 2048]
        O = big[:, 40960 + sub * 4096 + 2048:40960 + sub * 4096 + 4096]
        P.t.wait(S["dv"], sums_mark[sub])
        P.t.matmul(ps_s[:, 0:qw], C.onesrow_f[:, 0:128], r_sum,
                   start=True, stop=True).then_inc(S["pe"], 1)
        P.s.wait(S["pe"])
        P.s.wait(S["dv"], rb_rd[sub])
        P.s.copy(out=rb_sb, in_=ps_s[:, 0:qw]).then_inc(S["ac"], 1)
        b5_gate[sub] = S["ac"].n
        dst = (E if w % 2 == 0 else O)[:, q0:q0 + qw]
        P.v.wait(S["ac"])
        if dst_pl[sub] > 0:
            P.v.wait(S["pl"], dst_pl[sub])
        P.v.tensor_tensor(out=t_sb, in0=ps_av[:, 0:qw], in1=rb_sb,
                          op=ALU.mult).then_inc(S["dv"], 1)
        av_mark[sub] = S["dv"].n
        rb_rd[sub] = S["dv"].n
        P.g.wait(S["dv"], S["dv"].n)
        P.g.tensor_tensor(out=dst, in0=t_sb, in1=prof[:, 0:qw],
                          op=ALU.mult).then_inc(S["pl"], 1)
        dst_pl[sub] = S["pl"].n

    pe_entry_w = S["pe"].n
    for hp in range(16):
        for sub in range(2):
            h = 2 * hp + sub
            O = big[:, 40960 + sub * 4096 + 2048:40960 + sub * 4096 + 4096]
            P.y.wait(S["pe"], qh_mark[h % 4])
            P.y.dma_start(out=qh_of(h), in_=C.scr_q[:, h * 2048:(h + 1) * 2048]
                          ).then_inc(S["ldx"], 16)
            qv[sub] = S["ldx"].n
            if hp == 0:
                P.v.wait(S["pe"], pe_entry_w)
            P.v.memset(O[:, 0:256], 0.0).then_inc(S["dv"], 1)
        for w in range(8):
            hA, hB = 2 * hp, 2 * hp + 1
            acA = s1(hA, w)
            acB = s1(hB, w)
            s2(hA, w, acA)
            s2(hB, w, acB)
            s3(hA, w)
            s3(hB, w)
        for sub in range(2):
            h = 2 * hp + sub
            astg = big[:, 2048 + sub * 2048:2048 + (sub + 1) * 2048]
            E = big[:, 40960 + sub * 4096:40960 + sub * 4096 + 2048]
            O = big[:, 40960 + sub * 4096 + 2048:40960 + sub * 4096 + 4096]
            P.v.wait(S["lds"], astg_mark[sub])
            P.v.wait(S["pl"], dst_pl[sub])
            P.v.tensor_tensor(out=astg, in0=E, in1=O, op=ALU.add).then_inc(S["dv"], 1)
            P.s.wait(S["dv"])
            P.s.dma_start(out=C.scr_attn[:, h * 2048:(h + 1) * 2048], in_=astg
                          ).then_inc(S["lds"], 16)
            astg_mark[sub] = S["lds"].n
        qh_mark[(2 * hp) % 4] = S["pe"].n
        qh_mark[(2 * hp + 1) % 4] = S["pe"].n


def emit_l0_attn_b(C, final=False):
    pre = preload_w(C, "l0b_kblk", 4096, 1)
    load_rope_tables(C, 0)
    ln_into_big(C, C.dram["xT"])
    proj_loop(C, "l0b_kblk", None, 1, 4096,
              terms_plain(C, lambda dc, t0, t1: xn_ap(C, dc, t0, t1)),
              epi_rope_store(C, lambda oc: C.scr_kv[:, oc * 2048:(oc + 1) * 2048]),
              tt_gates=C.ln_tt_dv, pre=pre)
    emit_v_tm(C, "l0b_vblk", 1, 16384)
    proj_loop(C, "l0b_qblk", None, NCH, 4096,
              terms_plain(C, lambda dc, t0, t1: xn_ap(C, dc, t0, t1)),
              epi_rope_store(C, lambda oc: C.scr_q[:, oc * 2048:(oc + 1) * 2048]))
    emit_l0b_windows(C)
    _wo_pass(C, "l0b_oblk", final)


EMITTERS = {
    "bind_only": emit_bind_only,
    "l0_attn_b": emit_l0_attn_b,
    "l0_attn_a": emit_l0_attn_a,
    "l0_ffn": emit_l0_ffn,
    "l1_ffn": emit_l1_ffn,
    "l1_attn_a": emit_l1_attn_a,
    "l1_attn_b": emit_l1_attn_b,
}

SUBLAYERS_ALL = ["l0_attn_a", "l0_ffn", "l0_attn_b", "l0_ffn",
                 "l1_attn_a", "l1_ffn", "l1_attn_b", "l1_ffn"]



def host_constants():
    c = {}
    c["c_ident"] = np.eye(128, dtype=BF)
    P = np.zeros((128, 128), dtype=np.float32)
    for i in range(64):
        P[64 + i, i] = -1.0
        P[i, 64 + i] = 1.0
    c["c_prot"] = P.astype(BF)
    i_idx = np.arange(128)[:, None]
    c_idx = np.arange(896)[None, :]
    c["c_mask896"] = (i_idx <= c_idx - 384).astype(BF)
    c["c_ones128"] = np.ones((128, 1), dtype=BF)
    c["c_onesb"] = np.ones((1, 512), dtype=BF)
    c["c_onesf"] = np.ones((1, 512), dtype=np.float32)
    sel = np.zeros((16, 256), dtype=np.float32)
    for p in range(128):
        sel[p // 16, p] = 1.0
        sel[8 + p // 16, 128 + p] = 1.0
    c["c_sel"] = sel
    t = np.arange(T, dtype=np.float64)
    f = np.exp(-np.arange(0, 128, 2, dtype=np.float64) * (np.log(10000.0) / 128))
    ang1 = t[None, :] * f[:, None]
    c["c_cos1"] = np.concatenate([np.cos(ang1), np.cos(ang1)], 0).astype(BF)
    c["c_sin1"] = np.concatenate([np.sin(ang1), np.sin(ang1)], 0).astype(BF)
    ang0 = t[None, :] * f[:, None]
    cos0 = np.repeat(np.cos(ang0), 2, axis=0)
    sin0 = np.repeat(np.sin(ang0), 2, axis=0)
    c["c_cos0"] = cos0.astype(BF)
    c["c_sin0"] = sin0.astype(BF)
    gh = np.zeros((96, 96), dtype=np.float32)
    for c1 in range(96):
        for c2 in range(96):
            if c1 // 8 == c2 // 8:
                gh[c1, c2] = 1.0 / 1024.0
    c["c_ghead"] = gh
    al = (np.arange(256, dtype=np.float64) / 255.0)
    c["c_gmid"] = np.concatenate([al, 1.0 - al]).reshape(1, 512).astype(np.float32)
    c["c_gw0"] = np.concatenate([np.ones(256), 1.0 - al]).reshape(1, 512).astype(np.float32)
    c["c_gw7"] = al.reshape(1, 256).astype(np.float32)
    ii = np.arange(128)[:, None]
    cc = np.arange(2944)[None, :]
    Bm = (ii - cc + 384).astype(np.float32)
    c["c_balibi"] = np.minimum(Bm, 0.0)
    c["c_bmaskz"] = (Bm <= 0).astype(BF)
    return c


def host_weights(inp):
    w = {}
    W = np.asarray(inp["L0_ffn_W"], dtype=np.float32)
    Bf = np.asarray(inp["L0_ffn_Bf"], dtype=np.float32)
    A = np.asarray(inp["L0_ffn_A"], dtype=np.float32)
    wblk = lhsT_blocked(np.ascontiguousarray(W.T))
    M = np.ascontiguousarray(Bf.transpose(0, 2, 1).reshape(256, 4096))
    bfblk = lhsT_blocked(M)
    w["l0f_wblk"] = np.ascontiguousarray(np.concatenate([wblk, bfblk], axis=2))
    Ars = np.ascontiguousarray(A.reshape(256, 4096).T)
    w["l0f_ablk"] = lhsT_blocked(Ars)
    w["l0f_bias"] = fm_cols(np.asarray(inp["L0_ffn_bias"], dtype=np.float32))
    w["l0f_coeffs"] = np.asarray(inp["L0_ffn_coeffs"], np.float32).reshape(16, 1)
    w["l0f_scales"] = np.asarray(inp["L0_ffn_scales"], np.float32).reshape(16, 1)
    w["l1f_qblk"] = np.ascontiguousarray(
        np.asarray(inp["L1_ffn_qweight"], np.int32).reshape(NCH, 128, 3072))
    w["l1f_scale"] = fm_cols(np.asarray(inp["L1_ffn_weight_scale"], np.float32)[:, 0])
    w["l1f_zp"] = fm_cols(np.asarray(inp["L1_ffn_weight_zero_point"], np.float32)[:, 0])
    w["l1f_bias"] = fm_cols(np.asarray(inp["L1_ffn_bias"], np.float32))
    w["l1a_qblk"] = lhsT_blocked(np.ascontiguousarray(
        np.asarray(inp["L1_attn_a_Wq"], np.float32).T))
    w["l1a_kblk"] = lhsT_blocked(np.ascontiguousarray(
        np.asarray(inp["L1_attn_a_Wk"], np.float32).T))
    w["l1a_vblk"] = lhsT_blocked(np.ascontiguousarray(
        np.asarray(inp["L1_attn_a_Wv"], np.float32).T))
    w["l1a_oblk"] = lhsT_blocked(np.ascontiguousarray(
        np.asarray(inp["L1_attn_a_Wo"], np.float32).T))
    Wqkv = np.asarray(inp["L1_attn_b_Wqkv"], np.float32)
    w["l1b_qblk"] = lhsT_blocked(np.ascontiguousarray(Wqkv[:, 0:4096]))
    w["l1b_kblk"] = lhsT_blocked(np.ascontiguousarray(Wqkv[:, 4096:4224]))
    w["l1b_vblk"] = lhsT_blocked(np.ascontiguousarray(Wqkv[:, 4224:4352]))
    w["l1b_oblk"] = lhsT_blocked(np.asarray(inp["L1_attn_b_Wo"], np.float32))
    w["l0b_qblk"] = lhsT_blocked(np.ascontiguousarray(
        np.asarray(inp["L0_attn_b_Wq"], np.float32).T))
    w["l0b_kblk"] = lhsT_blocked(np.ascontiguousarray(
        np.asarray(inp["L0_attn_b_Wk"], np.float32).T))
    w["l0b_vblk"] = lhsT_blocked(np.ascontiguousarray(
        np.asarray(inp["L0_attn_b_Wv"], np.float32).T))
    w["l0b_oblk"] = lhsT_blocked(np.ascontiguousarray(
        np.asarray(inp["L0_attn_b_Wo"], np.float32).T))
    w["l0a_bvq"] = np.asarray(inp["L0_attn_a_bv_q"], np.float32)
    w["l0a_bvk"] = np.asarray(inp["L0_attn_a_bv_k"], np.float32)
    w["l0a_bvv"] = np.asarray(inp["L0_attn_a_bv_v"], np.float32)
    w.update(host_constants())
    return w


def weights_meta_from(maps):
    meta = {}
    for k, v in maps.items():
        dt = {np.dtype(np.float32): F32, np.dtype(BF): BF16,
              np.dtype(np.int32): I32}[v.dtype]
        meta[k] = (v.shape, dt)
    return meta


def run_subset(inputs, sublayers, n_cores=8, trace=False, dump=None):
    x = np.asarray(inputs["L0_attn_a_x"], dtype=np.float32)
    common = host_weights(inputs)
    xTs = [np.ascontiguousarray(x[b].T) for b in range(n_cores)]
    maps = [{**common, "xT": xTs[b]} for b in range(n_cores)]
    meta = weights_meta_from(maps[0])
    nc = build_nc(sublayers, meta, dump=dump)
    res = run_bass_kernel_spmd(nc, maps, core_ids=list(range(n_cores)), trace=trace)
    outs = [res.results[b]["out"] for b in range(n_cores)]
    y = np.stack([o.T for o in outs], axis=0).astype(np.float32)
    return y, res


def kernel(**inputs):
    y, _ = run_subset(inputs, SUBLAYERS_ALL, n_cores=8)
    return y


# version 17
# speedup vs baseline: 1.0283x; 1.0283x over previous
import numpy as np
import ml_dtypes
import concourse.bass as bass
import concourse.mybir as mybir
from concourse.bass_utils import run_bass_kernel_spmd

F32 = mybir.dt.float32
BF16 = mybir.dt.bfloat16
I32 = mybir.dt.int32
AF = mybir.ActivationFunctionType
ALU = mybir.AluOpType
BF = ml_dtypes.bfloat16

D = 4096
T = 2048
NCH = 32
NTT = 4
TT = 512


def lhsT_blocked(wt: np.ndarray) -> np.ndarray:
    d, o = wt.shape
    ndc, noc = d // 128, o // 128
    return np.ascontiguousarray(
        wt.reshape(ndc, 128, noc, 128).transpose(2, 1, 0, 3).reshape(noc, 128, ndc * 128))


def fm_cols(v: np.ndarray) -> np.ndarray:
    return np.ascontiguousarray(v.reshape(-1, 128).T)



ENGINES = ("tensor", "vector", "scalar", "gpsimd", "sync")


class Sem:
    __slots__ = ("name", "h", "n")

    def __init__(self, name):
        self.name, self.h, self.n = name, None, 0


class _RecInst:
    __slots__ = ("item",)

    def __init__(self, item):
        self.item = item

    def then_inc(self, sem: Sem, k: int):
        sem.n += k
        self.item[4].append((sem, k))
        return self


class _RecEng:
    def __init__(self, prog, name):
        self._prog, self._name = prog, name

    def __getattr__(self, meth):
        prog, name = self._prog, self._name

        def call(*a, **kw):
            item = [name, meth, a, kw, []]
            prog.ops.append(item)
            return _RecInst(item)
        return call

    def wait(self, sem: Sem, val=None):
        v = sem.n if val is None else val
        if v > 0:
            self._prog.ops.append([self._name, "wait_ge", (sem, v), {}, []])


class Prog:
    def __init__(self):
        self.ops = []
        self.t = _RecEng(self, "tensor")
        self.v = _RecEng(self, "vector")
        self.s = _RecEng(self, "scalar")
        self.g = _RecEng(self, "gpsimd")
        self.y = _RecEng(self, "sync")


def replay(P: Prog, name, eng):
    for e, meth, a, kw, incs in P.ops:
        if e != name:
            continue
        if meth == "wait_ge":
            sem, v = a
            eng.wait_ge(sem.h, v)
            continue
        inst = getattr(eng, meth)(*a, **kw)
        for sem, k in incs:
            inst = inst.then_inc(sem.h, k)



class Ctx:
    pass


def build_nc(sublayers, weights_meta, dump=None):
    nc = bass.Bass("TRN2", target_bir_lowering=False, debug=False)
    C = Ctx()
    C.nc = nc
    P = Prog()
    C.P = P
    C.dump = dump

    C.dram = {}
    for name, (shape, dt) in weights_meta.items():
        C.dram[name] = nc.dram_tensor(name, list(shape), dt, kind="ExternalInput").ap()
    C.out = nc.dram_tensor("out", [D, T], F32, kind="ExternalOutput").ap()
    C.scr_xn = nc.dram_tensor("scr_xn", [128, NCH * T], BF16).ap()
    C.scr_q = nc.dram_tensor("scr_q", [128, NCH * T], BF16).ap()
    C.scr_attn = nc.dram_tensor("scr_attn", [128, NCH * T], BF16).ap()
    C.scr_kv = nc.dram_tensor("scr_kv", [128, 24 * T], BF16).ap()
    C.scr_w1 = nc.dram_tensor("scr_w1", [NCH, 128, NCH * 128], BF16).ap()
    C.scr_misc = nc.dram_tensor("scr_misc", [1, 65536], F32).ap()
    if dump is not None:
        C.out_dbg = nc.dram_tensor("out_dbg", [128, 65536], BF16,
                                   kind="ExternalOutput").ap()

    C.sem = {s: Sem(s) for s in ("ldx", "ldw", "lds", "st", "pe", "ac", "dv", "pl")}
    C.xbf_marks = {}

    import contextlib
    stack = contextlib.ExitStack()
    with stack:
        sb = {}

        def alloc(name, shape, dt):
            sb[name] = stack.enter_context(nc.sbuf_tensor(name, shape, dt))

        alloc("BIG", [128, 65536], BF16)
        alloc("MED", [128, 8192], BF16)
        alloc("WB", [128, 8704], BF16)
        alloc("BF32", [128, 7168], F32)
        alloc("SF32", [128, 512], F32)
        alloc("PRM", [128, 1280], BF16)
        alloc("ROWS", [1, 3072], F32)
        alloc("ROWSB", [1, 512], BF16)
        C.sb = sb

        ps = {}
        for i in range(6):
            ps[i] = stack.enter_context(nc.psum_tensor(f"ps{i}", [128, 512], F32))
        ps[6] = stack.enter_context(nc.psum_tensor("ps6b", [128, 1024], BF16))
        ps[7] = stack.enter_context(nc.psum_tensor("ps7b", [128, 1024], BF16))
        C.ps = ps

        for s in C.sem:
            C.sem[s].h = stack.enter_context(nc.semaphore(s))

        C.ident = sb["PRM"][:, 0:128]
        C.prot = sb["PRM"][:, 128:256]
        C.mask896 = sb["PRM"][:, 256:1152]
        C.ones128 = sb["PRM"][:, 1152:1153]
        C.onesrow_b = sb["ROWSB"][:, 0:512]
        C.onesrow_f = sb["ROWS"][:, 0:512]

        emit_prologue(C)
        for i, sl in enumerate(sublayers):
            EMITTERS[sl](C, final=(i == len(sublayers) - 1))
        emit_epilogue(C)

        with nc.Block() as block:
            for eng_name in ENGINES:
                def mk(nm):
                    def body(eng):
                        replay(P, nm, eng)
                    return body
                getattr(block, eng_name)(mk(eng_name))
    return nc



def emit_prologue(C):
    P, sb, dram, S, ps = C.P, C.sb, C.dram, C.sem, C.ps
    for dst, src in [
        (sb["PRM"][:, 0:128], "c_ident"),
        (sb["PRM"][:, 128:256], "c_prot"),
        (sb["PRM"][:, 256:1152], "c_mask896"),
        (sb["PRM"][:, 1152:1153], "c_ones128"),
        (sb["ROWSB"][:, 0:512], "c_onesb"),
        (sb["ROWS"][:, 0:512], "c_onesf"),
        (sb["SF32"][:, 0:32], "l0f_bias"),
        (sb["SF32"][:, 32:64], "l1f_bias"),
        (sb["SF32"][:, 64:96], "l1f_scale"),
        (sb["SF32"][:, 96:128], "l1f_zp"),
        (sb["SF32"][0:16, 132:388], "c_sel"),
        (sb["SF32"][0:16, 130:131], "l0f_coeffs"),
        (sb["SF32"][0:16, 131:132], "l0f_scales"),
    ]:
        P.y.dma_start(out=dst, in_=dram[src]).then_inc(S["ldx"], 16)
    P.v.wait(S["ldx"])
    P.v.tensor_tensor(out=sb["SF32"][0:16, 130:131], in0=sb["SF32"][0:16, 130:131],
                      in1=sb["SF32"][0:16, 131:132], op=ALU.mult).then_inc(S["dv"], 1)
    P.t.wait(S["dv"])
    P.t.matmul(ps[4][:, 0:1], sb["SF32"][0:16, 132:260], sb["SF32"][0:16, 130:131],
               start=True, stop=True)
    mm = P.t.matmul(ps[4][:, 1:2], sb["SF32"][0:16, 260:388], sb["SF32"][0:16, 130:131],
                    start=True, stop=True)
    mm.then_inc(S["pe"], 1)
    P.v.wait(S["pe"])
    P.v.tensor_copy(out=sb["SF32"][:, 128:130], in_=ps[4][:, 0:2]).then_inc(S["dv"], 1)


def emit_epilogue(C):
    P, S = C.P, C.sem
    P.y.wait(S["st"])
    P.y.wait(S["lds"])
    P.y.wait(S["ldw"])
    if getattr(C, "dump", None) is not None:
        P.y.wait(S["dv"])
        P.y.wait(S["ac"])
        P.y.wait(S["pe"])
        if C.dump == "scr_misc":
            P.g.wait(S["st"])
            P.g.wait(S["lds"])
            P.g.dma_start(out=C.out_dbg[0:1, 0:65536], in_=C.scr_misc
                          ).then_inc(S["ldx"], 16)
        else:
            src = {"scr_q": C.scr_q, "scr_kv": C.scr_kv, "scr_attn": C.scr_attn,
                   "scr_xn": C.scr_xn}[C.dump]
            n = src.shape[1]
            P.y.dma_start(out=C.out_dbg[:, 0:n], in_=src).then_inc(S["ldx"], 16)
    P.y.wait(S["ldx"])



def ln_into_big(C, x_src):
    P, sb, S, ps = C.P, C.sb, C.sem, C.ps
    xr = x_src.rearrange("(c p) t -> p c t", p=128)
    bigr = sb["BIG"][:, 0:65536].rearrange("p (c t) -> p c t", t=T)
    pe_entry = S["pe"].n
    dv_entry = S["dv"].n
    C.ln_dv_entry = dv_entry
    C.ln_ac_entry = S["ac"].n
    xbfr = C.scr_xn.rearrange("p (c t) -> p c t", t=T)
    P.y.wait(S["pe"], pe_entry)
    P.y.wait(S["dv"], dv_entry)
    ldx_of = []
    for cb in range(8):
        P.y.wait(S["lds"], C.xbf_marks.get(4 * cb + 3, 0))
        P.y.dma_start(out=bigr[:, 4 * cb:4 * cb + 4, :],
                      in_=xbfr[:, 4 * cb:4 * cb + 4, :]).then_inc(S["ldx"], 16)
        ldx_of.append(S["ldx"].n)
    P.s.wait(S["pe"], pe_entry)
    P.s.wait(S["lds"], getattr(C, "med_lds", 0))
    x2_rd = [pe_entry, pe_entry]
    C.ln_tt_dv = []
    rb_of = [sb["BF32"][:, 4096:4608], sb["BF32"][:, 5120:5632]]
    mrb_of = [sb["BF32"][:, 4608:5120], sb["BF32"][:, 5632:6144]]
    bank_free = [(S["dv"], dv_entry), (S["dv"], dv_entry)]
    stat_pe = [0] * NTT

    def emit_stats(tt):
        par = tt % 2
        pA, pB = (ps[4], ps[5]) if par == 0 else (ps[2], ps[3])
        P.t.wait(*bank_free[par])
        for cb in range(8):
            sl = cb % 2
            x2 = sb["MED"][:, 4096 + sl * 2048:4096 + (sl + 1) * 2048]
            x23 = x2.rearrange("p (c t) -> p c t", t=TT)
            P.s.wait(S["ldx"], ldx_of[cb])
            P.s.wait(S["pe"], x2_rd[sl])
            P.s.activation(out=x23, in_=bigr[:, 4 * cb:4 * cb + 4,
                                            tt * TT:(tt + 1) * TT],
                           func=AF.Square).then_inc(S["ac"], 1)
            if tt == 0:
                P.t.wait(S["ldx"], ldx_of[cb])
            first = (cb == 0)
            for k in range(4):
                P.t.matmul(pA[0:1, :], C.ones128,
                           xn_ap(C, 4 * cb + k, tt * TT, (tt + 1) * TT),
                           start=(first and k == 0), stop=(cb == 7 and k == 3))
            P.t.wait(S["ac"])
            for k in range(4):
                mm = P.t.matmul(pB[0:1, :], C.ones128, x2[:, k * TT:(k + 1) * TT],
                                start=(first and k == 0), stop=(cb == 7 and k == 3))
            mm.then_inc(S["pe"], 1)
            x2_rd[sl] = S["pe"].n
        stat_pe[tt] = S["pe"].n

    emit_stats(0)
    last_norm_par = [[], []]
    rows_done = [None] * NTT

    def emit_rows(tt, prev_pe_bc=0, prev_rows_dv=0):
        pA, pB = (ps[4], ps[5]) if tt % 2 == 0 else (ps[2], ps[3])
        r_mean = sb["ROWS"][:, 512:1024]
        r_var = sb["ROWS"][:, 1024:1536]
        r_m2 = sb["ROWS"][:, 1536:2048]
        r_rstd = sb["ROWS"][:, 2048:2560]
        r_mr = sb["ROWS"][:, 2560:3072]
        if prev_rows_dv > 0:
            P.s.wait(S["dv"], prev_rows_dv)
        if prev_pe_bc > 0:
            P.s.wait(S["pe"], prev_pe_bc)
        P.s.wait(S["pe"], stat_pe[tt])
        P.s.activation(out=r_mean, in_=pA[0:1, :], func=AF.Copy,
                       scale=np.float32(1.0 / D)).then_inc(S["ac"], 1)
        P.s.activation(out=r_var, in_=pB[0:1, :], func=AF.Copy,
                       scale=np.float32(1.0 / D)).then_inc(S["ac"], 1)
        P.s.activation(out=r_m2, in_=r_mean, func=AF.Square).then_inc(S["ac"], 1)
        P.v.wait(S["ac"], S["ac"].n)
        P.v.tensor_tensor(out=r_var, in0=r_var, in1=r_m2, op=ALU.subtract
                          ).then_inc(S["dv"], 1)
        P.s.wait(S["dv"], S["dv"].n)
        P.s.activation(out=r_var, in_=r_var, func=AF.Ln, bias=np.float32(1e-5)).then_inc(S["ac"], 1)
        P.s.activation(out=r_rstd, in_=r_var, func=AF.Exp, scale=np.float32(-0.5)
                       ).then_inc(S["ac"], 1)
        ac_rstd = S["ac"].n
        P.v.wait(S["ac"], ac_rstd)
        P.v.tensor_tensor(out=r_mr, in0=r_mean, in1=r_rstd, op=ALU.mult
                          ).then_inc(S["dv"], 1)
        rows_done[tt] = (ac_rstd, S["dv"].n, r_rstd, r_mr)

    emit_rows(0)
    for tt in range(NTT):
        par = tt % 2
        pA, pB = (ps[4], ps[5]) if par == 0 else (ps[2], ps[3])
        if tt + 1 < NTT:
            emit_stats(tt + 1)
        ac_rstd, rows_dv, r_rstd, r_mr = rows_done[tt]
        P.t.wait(S["ac"], ac_rstd)
        P.t.wait(S["dv"], rows_dv)
        P.t.matmul(pA[:, :], C.onesrow_f[:, 0:128], r_rstd, start=True, stop=True)
        mm = P.t.matmul(pB[:, :], C.onesrow_f[:, 0:128], r_mr, start=True, stop=True)
        mm.then_inc(S["pe"], 1)
        pe_bc = S["pe"].n
        P.s.wait(S["pe"], pe_bc)
        for sem, val in last_norm_par[par]:
            P.s.wait(sem, val)
        P.s.copy(out=rb_of[par], in_=pA[:, :]).then_inc(S["ac"], 1)
        P.s.copy(out=mrb_of[par], in_=pB[:, :]).then_inc(S["ac"], 1)
        evac_ac = S["ac"].n
        if tt + 1 < NTT:
            emit_rows(tt + 1, pe_bc, rows_dv)
        bank_free[par] = (S["ac"], evac_ac)
        t1v = sb["BF32"][:, 6144:6656]
        t1g = sb["BF32"][:, 6656:7168]
        first_v, first_g = True, True
        for c in range(NCH):
            xv = xn_ap(C, c, tt * TT, (tt + 1) * TT)
            if c % 4 != 3:
                if first_v:
                    P.v.wait(S["ac"], evac_ac)
                    P.v.wait(S["ldx"], ldx_of[7])
                    P.v.wait(S["pe"], pe_bc)
                    first_v = False
                P.v.tensor_tensor(out=t1v, in0=xv, in1=rb_of[par], op=ALU.mult
                                  ).then_inc(S["dv"], 1)
                P.v.tensor_tensor(out=xv, in0=t1v, in1=mrb_of[par], op=ALU.subtract
                                  ).then_inc(S["dv"], 1)
            else:
                if first_g:
                    P.g.wait(S["ac"], evac_ac)
                    P.g.wait(S["ldx"], ldx_of[7])
                    P.g.wait(S["pe"], pe_bc)
                    first_g = False
                P.g.tensor_tensor(out=t1g, in0=xv, in1=rb_of[par], op=ALU.mult
                                  ).then_inc(S["pl"], 1)
                P.g.tensor_tensor(out=xv, in0=t1g, in1=mrb_of[par], op=ALU.subtract
                                  ).then_inc(S["pl"], 1)
        last_norm_par[par] = [(S["dv"], S["dv"].n), (S["pl"], S["pl"].n)]
        C.ln_tt_dv.append(list(last_norm_par[par]))


def xn_ap(C, c, t0, t1):
    return C.sb["BIG"][:, c * T + t0:c * T + t1]



def preload_w(C, wname, wcols, noc, wq_engine="g"):
    P, sb, S = C.P, C.sb, C.sem
    if not hasattr(C, "wb_mark"):
        C.wb_mark = [0, 0]
    if getattr(C, "wb_wcols", None) != wcols:
        m = max(C.wb_mark)
        C.wb_mark = [m, m]
        C.wb_wcols = wcols
    weng = getattr(P, wq_engine)
    wv_of = {}
    for oc in range(min(2, noc)):
        sl = oc % 2
        weng.wait(S["pe"], C.wb_mark[sl])
        weng.dma_start(out=sb["WB"][:, sl * wcols:(sl + 1) * wcols],
                       in_=C.dram[wname][oc] if isinstance(wname, str) else wname[oc]
                       ).then_inc(S["ldw"], 16)
        wv_of[oc] = S["ldw"].n
    return wv_of


def proj_loop(C, wname, rhs_fn, noc, wcols, terms_fn, epi_fn, cast=True,
              wq_engine="g", first_waits=(), tt_gates=None, pre=None):
    P, sb, S, ps = C.P, C.sb, C.sem, C.ps
    if not hasattr(C, "wb_mark"):
        C.wb_mark = [0, 0]
    if getattr(C, "wb_wcols", None) != wcols:
        m = max(C.wb_mark)
        C.wb_mark = [m, m]
        C.wb_wcols = wcols
    if tt_gates is not None:
        g0 = (S["dv"], C.ln_dv_entry)
        a0 = (S["ac"], C.ln_ac_entry)
    else:
        g0 = (S["dv"], S["dv"].n)
        a0 = (S["ac"], S["ac"].n)
    bank_gate = [g0, g0, g0, g0]
    first_gate_extra = a0
    weng = getattr(P, wq_engine)
    wsem = S["ldw"]
    wv_of = {}
    if pre is not None:
        wv_of.update(pre)

    def emit_load(oc):
        sl = oc % 2
        weng.wait(S["pe"], C.wb_mark[sl])
        weng.dma_start(out=sb["WB"][:, sl * wcols:(sl + 1) * wcols],
                       in_=C.dram[wname][oc] if isinstance(wname, str) else wname[oc]
                       ).then_inc(wsem, 16)
        wv_of[oc] = wsem.n

    if 0 not in wv_of:
        emit_load(0)
    if noc > 1 and 1 not in wv_of:
        emit_load(1)
    flush = getattr(epi_fn, "flush", None)
    for oc in range(noc):
        sl = oc % 2
        for tt in range(NTT):
            g = oc * NTT + tt
            b = g % 4
            if tt == 0:
                P.t.wait(wsem, wv_of[oc])
            if oc == 0 and tt == 0:
                if first_gate_extra[1] > 0:
                    P.t.wait(*first_gate_extra)
                for fw in first_waits:
                    P.t.wait(*fw)
            if oc == 0 and tt_gates is not None:
                for sem, val in tt_gates[tt]:
                    P.t.wait(sem, val)
            if bank_gate[b] is not None and bank_gate[b][1] > 0:
                P.t.wait(*bank_gate[b])
            wb = sb["WB"][:, sl * wcols:(sl + 1) * wcols]
            pairs = terms_fn(oc, tt, wb)
            for j, (lh, rh) in enumerate(pairs):
                mm = P.t.matmul(ps[b][:, :], lh, rh,
                                start=(j == 0), stop=(j == len(pairs) - 1))
            mm.then_inc(S["pe"], 1)
            if tt == NTT - 1:
                C.wb_mark[sl] = S["pe"].n
            bank_gate[b] = epi_fn(oc, tt, ps[b][:, :])
        if oc + 2 < noc:
            emit_load(oc + 2)
        if flush is not None:
            flush(oc)


def terms_plain(C, xsrc_fn):
    def f(oc, tt, wb):
        return [(wb[:, dc * 128:(dc + 1) * 128], xsrc_fn(dc, tt * TT, (tt + 1) * TT))
                for dc in range(NCH)]
    return f


def terms_lowrank(C, xsrc_fn, xa_sb):
    def f(oc, tt, wb):
        pairs = [(wb[:, dc * 128:(dc + 1) * 128], xsrc_fn(dc, tt * TT, (tt + 1) * TT))
                 for dc in range(NCH)]
        for kc in range(2):
            pairs.append((wb[:, 4096 + kc * 128:4096 + (kc + 1) * 128],
                          xa_sb[:, kc * T + tt * TT:kc * T + (tt + 1) * TT]))
        return pairs
    return f


def epi_accum(C, bias_fn):
    P, sb, S = C.P, C.sb, C.sem
    st_mark = [0, 0]

    def f(oc, tt, psum):
        sl = oc % 2
        yt = sb["BF32"][:, sl * 2048 + tt * TT: sl * 2048 + (tt + 1) * TT]
        P.v.wait(S["pe"])
        if tt == 0:
            P.v.wait(S["st"], st_mark[sl])
        bcol = bias_fn(oc)
        if bcol is not None:
            P.v.tensor_scalar(out=yt, in0=psum, scalar1=bcol, scalar2=None,
                              op0=ALU.add).then_inc(S["dv"], 1)
        else:
            P.v.tensor_copy(out=yt, in_=psum).then_inc(S["dv"], 1)
        return (S["dv"], S["dv"].n)

    def flush(oc):
        sl = oc % 2
        if not hasattr(C, "accum_marks"):
            C.accum_marks = {}
        P.g.wait(S["dv"])
        P.g.dma_start(out=C.dram["xT"][oc * 128:(oc + 1) * 128, :],
                      in_=sb["BF32"][:, sl * 2048:sl * 2048 + 2048],
                      accum_op=ALU.add).then_inc(S["st"], 16)
        st_mark[sl] = S["st"].n
        C.accum_marks[oc] = S["st"].n
        P.g.wait(S["st"], S["st"].n)
        P.g.dma_start(out=C.scr_xn[:, oc * T:(oc + 1) * T],
                      in_=C.dram["xT"][oc * 128:(oc + 1) * 128, :]
                      ).then_inc(S["lds"], 16)
        C.xbf_marks[oc] = S["lds"].n
    f.flush = flush
    return f


def epi_final(C, bias_fn):
    P, sb, S = C.P, C.sb, C.sem
    ld_mark = [S["dv"].n]
    pl_mark = [S["pl"].n]
    st_mark = [0, 0]

    def f(oc, tt, psum):
        sl = oc % 2
        yt = sb["BF32"][:, sl * 2048 + tt * TT:sl * 2048 + (tt + 1) * TT]
        xrow = sb["BF32"][:, 4096 + tt * TT:4096 + (tt + 1) * TT]
        if tt == 0:
            P.y.wait(S["dv"], ld_mark[0])
            if pl_mark[0] > 0:
                P.y.wait(S["pl"], pl_mark[0])
                pl_mark[0] = 0
            P.y.dma_start(out=sb["BF32"][:, 4096:6144],
                          in_=C.dram["xT"][oc * 128:(oc + 1) * 128, :]
                          ).then_inc(S["ldx"], 16)
        P.v.wait(S["pe"])
        P.v.wait(S["ldx"])
        if tt == 0:
            P.v.wait(S["lds"], st_mark[sl])
        bcol = bias_fn(oc)
        if bcol is not None:
            P.v.tensor_scalar(out=yt, in0=psum, scalar1=bcol, scalar2=None,
                              op0=ALU.add).then_inc(S["dv"], 1)
        else:
            P.v.tensor_copy(out=yt, in_=psum).then_inc(S["dv"], 1)
        P.v.tensor_tensor(out=yt, in0=yt, in1=xrow, op=ALU.add).then_inc(S["dv"], 1)
        if tt == NTT - 1:
            ld_mark[0] = S["dv"].n
        return (S["dv"], S["dv"].n)

    def flush(oc):
        sl = oc % 2
        P.s.wait(S["dv"])
        P.s.dma_start(out=C.out[oc * 128:(oc + 1) * 128, :],
                      in_=sb["BF32"][:, sl * 2048:sl * 2048 + 2048]
                      ).then_inc(S["lds"], 16)
        st_mark[sl] = S["lds"].n
    f.flush = flush
    return f



def emit_xa(C, pre=None):
    P, sb, S, ps = C.P, C.sb, C.sem, C.ps
    C.xa_tt_dv = [None, None, None, None]

    def epi(oc, tt, psum):
        P.v.wait(S["pe"])
        P.v.tensor_scalar(out=sb["MED"][:, oc * T + tt * TT:oc * T + (tt + 1) * TT],
                          in0=psum, scalar1=sb["SF32"][:, 128 + oc:129 + oc],
                          scalar2=None, op0=ALU.mult).then_inc(S["dv"], 1)
        if oc == 1:
            C.xa_tt_dv[tt] = [(S["dv"], S["dv"].n)]
        return (S["dv"], S["dv"].n)

    proj_loop(C, "l0f_ablk", None, 2, 4096,
              terms_plain(C, lambda dc, t0, t1: xn_ap(C, dc, t0, t1)), epi,
              tt_gates=C.ln_tt_dv, pre=pre)


def emit_l0_ffn(C, final=False):
    pre = preload_w(C, "l0f_ablk", 4096, 2)
    ln_into_big(C, C.dram["xT"])
    emit_xa(C, pre=pre)
    terms = terms_lowrank(C, lambda dc, t0, t1: xn_ap(C, dc, t0, t1),
                          C.sb["MED"][:, 0:4096])
    bias_fn = lambda oc: C.sb["SF32"][:, oc:oc + 1]
    epi = epi_final(C, bias_fn) if final else epi_accum(C, bias_fn)
    proj_loop(C, "l0f_wblk", None, NCH, 4352, terms, epi, tt_gates=C.xa_tt_dv)


def emit_prep_w1(C):
    if getattr(C, "_w1_prepped", False):
        return
    C._w1_prepped = True
    P, sb, S, ps = C.P, C.sb, C.sem, C.ps
    P.y.wait(S["st"])
    P.y.wait(S["dv"])
    P.v.wait(S["st"])
    qi = sb["BF32"][:, 0:3072].bitcast(I32)
    Uh = [sb["BF32"][:, 3072:5120].bitcast(I32),
          sb["BF32"][:, 5120:7168].bitcast(I32)]
    wbfh = [sb["MED"][:, 0:2048], sb["MED"][:, 2048:4096]]
    stg = sb["MED"][:, 4096:8192]
    q_mark = 0
    wbf_rd = [S["pe"].n, S["pe"].n]
    stg_mark = 0
    bank_mark = [S["ac"].n, S["dv"].n]
    aff_dv = [0, 0]
    for nc_ in range(NCH):
        P.y.wait(S["dv"], q_mark)
        P.y.dma_start(out=qi, in_=C.dram["l1f_qblk"][nc_]).then_inc(S["ldx"], 16)
        P.v.wait(S["ldx"])
        for h in range(2):
            q3 = qi[:, 1536 * h:1536 * (h + 1)].rearrange("p (g c) -> p g c", c=3)
            U4 = Uh[h].rearrange("p (g j) -> p g j", j=4)
            P.v.tensor_scalar(out=U4[:, :, 0:1], in0=q3[:, :, 0:1], scalar1=63,
                              scalar2=None, op0=ALU.bitwise_and).then_inc(S["dv"], 1)
            P.v.tensor_scalar(out=U4[:, :, 1:2], in0=q3[:, :, 0:1], scalar1=6,
                              scalar2=63, op0=ALU.logical_shift_right,
                              op1=ALU.bitwise_and).then_inc(S["dv"], 1)
            P.v.tensor_scalar(out=U4[:, :, 2:3], in0=q3[:, :, 1:2], scalar1=63,
                              scalar2=None, op0=ALU.bitwise_and).then_inc(S["dv"], 1)
            P.v.tensor_scalar(out=U4[:, :, 3:4], in0=q3[:, :, 2:3], scalar1=63,
                              scalar2=None, op0=ALU.bitwise_and).then_inc(S["dv"], 1)
            if h == 1:
                q_mark = S["dv"].n
            P.v.wait(S["pe"], wbf_rd[h])
            P.v.tensor_scalar(out=wbfh[h], in0=Uh[h],
                              scalar1=sb["SF32"][:, 96 + nc_:97 + nc_],
                              scalar2=sb["SF32"][:, 64 + nc_:65 + nc_],
                              op0=ALU.subtract, op1=ALU.mult).then_inc(S["dv"], 1)
            aff_dv[h] = S["dv"].n
        for h in range(2):
            P.t.wait(S["dv"], aff_dv[h])
            for k in range(16):
                kk = 16 * h + k
                par = kk % 2
                pb = ps[6 + par]
                P.t.wait(S["ac"] if par == 0 else S["dv"], bank_mark[par])
                mm = P.t.transpose(pb[:, 0:128], wbfh[h][:, k * 128:(k + 1) * 128],
                                   C.ident)
                mm.then_inc(S["pe"], 1)
                eng = P.s if par == 0 else P.v
                eng.wait(S["pe"])
                if kk < 2:
                    eng.wait(S["lds"], stg_mark)
                if par == 0:
                    P.s.copy(out=stg[:, kk * 128:(kk + 1) * 128], in_=pb[:, 0:128]
                             ).then_inc(S["ac"], 1)
                    bank_mark[0] = S["ac"].n
                else:
                    P.v.tensor_copy(out=stg[:, kk * 128:(kk + 1) * 128],
                                    in_=pb[:, 0:128]).then_inc(S["dv"], 1)
                    bank_mark[1] = S["dv"].n
            wbf_rd[h] = S["pe"].n
        P.s.wait(S["dv"], bank_mark[1])
        P.s.dma_start(out=C.scr_w1[nc_], in_=stg).then_inc(S["lds"], 16)
        stg_mark = S["lds"].n
        C.med_lds = stg_mark


def emit_l1_ffn(C, final=False):
    emit_prep_w1(C)
    pre = preload_w(C, C.scr_w1, 4096, NCH, wq_engine="s")
    ln_into_big(C, C.dram["xT"])
    terms = terms_plain(C, lambda dc, t0, t1: xn_ap(C, dc, t0, t1))
    bias_fn = lambda oc: C.sb["SF32"][:, 32 + oc:33 + oc]
    epi = epi_final(C, bias_fn) if final else epi_accum(C, bias_fn)
    proj_loop(C, C.scr_w1, None, NCH, 4096, terms, epi, cast=False,
              wq_engine="s", tt_gates=C.ln_tt_dv, pre=pre)



def load_rope_tables(C, which):
    P, sb, S = C.P, C.sb, C.sem
    P.y.wait(S["dv"])
    P.y.wait(S["pe"])
    P.y.dma_start(out=sb["MED"][:, 0:2048], in_=C.dram[f"c_cos{which}"]
                  ).then_inc(S["ldx"], 16)
    P.y.dma_start(out=sb["MED"][:, 2048:4096], in_=C.dram[f"c_sin{which}"]
                  ).then_inc(S["ldx"], 16)


def epi_rope_store(C, scr_dst_fn):
    P, sb, S, ps = C.P, C.sb, C.sem, C.ps
    rot_mark = [S["dv"].n, S["dv"].n]
    asm_mark = [0]
    kb_mark = [S["pe"].n, S["pe"].n]
    kb_dv = [0, 0]

    def f(oc, tt, psum):
        g = oc * NTT + tt
        sl = g % 2
        kb = sb["MED"][:, 4096 + sl * 512:4096 + (sl + 1) * 512]
        t_s = sb["MED"][:, 5120 + sl * 512:5632 + sl * 512]
        asm = sb["MED"][:, 6144 + tt * 512:6144 + (tt + 1) * 512]
        cos = sb["MED"][:, tt * 512:(tt + 1) * 512]
        sin = sb["MED"][:, 2048 + tt * 512:2048 + (tt + 1) * 512]
        pr = ps[4 + sl]
        P.s.wait(S["pe"])
        P.s.wait(S["pe"], kb_mark[sl])
        if kb_dv[sl] > 0:
            P.s.wait(S["dv"], kb_dv[sl])
        P.s.activation(out=kb, in_=psum, func=AF.Copy).then_inc(S["ac"], 1)
        gate = (S["ac"], S["ac"].n)
        P.t.wait(S["ac"])
        P.t.wait(S["dv"], rot_mark[sl])
        P.t.matmul(pr[:, :], C.prot, kb, start=True, stop=True
                   ).then_inc(S["pe"], 1)
        kb_mark[sl] = S["pe"].n
        P.v.wait(S["pe"])
        if tt == 0:
            P.v.wait(S["lds"], asm_mark[0])
        P.v.tensor_tensor(out=asm, in0=kb, in1=cos, op=ALU.mult).then_inc(S["dv"], 1)
        kb_dv[sl] = S["dv"].n
        P.v.tensor_tensor(out=t_s, in0=pr[:, :], in1=sin, op=ALU.mult
                          ).then_inc(S["dv"], 1)
        rot_mark[sl] = S["dv"].n
        P.v.tensor_tensor(out=asm, in0=asm, in1=t_s, op=ALU.add).then_inc(S["dv"], 1)
        return gate

    def flush(oc):
        P.s.wait(S["dv"])
        P.s.dma_start(out=scr_dst_fn(oc), in_=sb["MED"][:, 6144:8192]
                      ).then_inc(S["lds"], 16)
        asm_mark[0] = S["lds"].n
        C.med_lds = asm_mark[0]
    f.flush = flush
    return f


def emit_v_tm(C, vblk_name, nvq, scr_col0):
    P, sb, S, ps = C.P, C.sb, C.sem, C.ps
    if not hasattr(C, "wb_mark"):
        C.wb_mark = [0, 0]
    if getattr(C, "wb_wcols", None) != 4096:
        m = max(C.wb_mark)
        C.wb_mark = [m, m]
        C.wb_wcols = 4096
    bank_gate = {}
    g0 = (S["dv"], S["dv"].n)
    stg_mark = [0, 0]
    wv_of = {}

    def emit_load(vq):
        sl = vq % 2
        P.g.wait(S["pe"], C.wb_mark[sl])
        P.g.dma_start(out=sb["WB"][:, sl * 4096:(sl + 1) * 4096],
                      in_=C.dram[vblk_name][vq]).then_inc(S["ldw"], 16)
        wv_of[vq] = S["ldw"].n

    emit_load(0)
    if nvq > 1:
        emit_load(1)
    for vq in range(nvq):
        sl = vq % 2
        stg = sb["MED"][:, 4096 + sl * 2048:4096 + (sl + 1) * 2048]
        for tc in range(16):
            b = tc % 4
            if tc == 0:
                P.t.wait(S["ldw"], wv_of[vq])
            gate = bank_gate.get(b, g0)
            if gate[1] > 0:
                P.t.wait(*gate)
            for dc in range(NCH):
                mm = P.t.matmul(ps[b][:, 0:128],
                                xn_ap(C, dc, 0, T)[:, tc * 128:(tc + 1) * 128],
                                sb["WB"][:, sl * 4096 + dc * 128:sl * 4096 + (dc + 1) * 128],
                                start=(dc == 0), stop=(dc == NCH - 1))
            mm.then_inc(S["pe"], 1)
            if tc == 15:
                C.wb_mark[sl] = S["pe"].n
            P.v.wait(S["pe"])
            if tc == 0:
                P.v.wait(S["lds"], stg_mark[sl])
            P.v.tensor_copy(out=stg[:, tc * 128:(tc + 1) * 128], in_=ps[b][:, 0:128]
                            ).then_inc(S["dv"], 1)
            bank_gate[b] = (S["dv"], S["dv"].n)
        if vq + 2 < nvq:
            emit_load(vq + 2)
        P.s.wait(S["dv"])
        dst = C.scr_kv[:, scr_col0:scr_col0 + 16 * nvq * 128].rearrange(
            "p (tc v) -> p tc v", v=nvq * 128)[:, :, vq * 128:(vq + 1) * 128]
        P.s.dma_start(out=dst, in_=stg.rearrange("p (tc o) -> p tc o", o=128)
                      ).then_inc(S["lds"], 16)
        stg_mark[sl] = S["lds"].n
        C.med_lds = stg_mark[sl]


def emit_l1_attn_a(C, final=False):
    P, sb, S, ps = C.P, C.sb, C.sem, C.ps
    pre = preload_w(C, "l1a_kblk", 4096, 8)
    load_rope_tables(C, 1)
    ln_into_big(C, C.dram["xT"])
    proj_loop(C, "l1a_kblk", None, 8, 4096,
              terms_plain(C, lambda dc, t0, t1: xn_ap(C, dc, t0, t1)),
              epi_rope_store(C, lambda oc: C.scr_kv[:, oc * 2048:(oc + 1) * 2048]),
              tt_gates=C.ln_tt_dv, pre=pre)
    emit_v_tm(C, "l1a_vblk", 8, 16384)
    proj_loop(C, "l1a_qblk", None, NCH, 4096,
              terms_plain(C, lambda dc, t0, t1: xn_ap(C, dc, t0, t1)),
              epi_rope_store(C, lambda oc: C.scr_q[:, oc * 2048:(oc + 1) * 2048]))
    emit_attn_core(C, nheads=32, kv_of=lambda h: h // 4,
                   kr_cols=lambda kvh, kc: 0 + kvh * 2048 + kc * 128,
                   vt_cols=lambda kvh, kc: 16384 + kc * 1024 + kvh * 128,
                   scale=1.0 / np.sqrt(128.0), kind="softmax")
    _wo_pass(C, "l1a_oblk", final)


def _wo_pass(C, oblk, final, bias_fn=None):
    P, sb, S = C.P, C.sb, C.sem
    P.y.wait(S["lds"])
    P.y.wait(S["pe"])
    P.y.wait(S["dv"])
    P.y.dma_start(out=sb["BIG"][:, 0:65536], in_=C.scr_attn).then_inc(S["ldx"], 16)
    ldx_big = S["ldx"].n
    terms = terms_plain(C, lambda dc, t0, t1: xn_ap(C, dc, t0, t1))
    if bias_fn is None:
        bias_fn = lambda oc: None
    epi = epi_final(C, bias_fn) if final else epi_accum(C, bias_fn)
    proj_loop(C, oblk, None, NCH, 4096, terms, epi,
              first_waits=((C.sem["ldx"], ldx_big),))


def emit_attn_core(C, nheads, kv_of, kr_cols, vt_cols, scale, kind,
                   alibi=None, kv_cols=(16384, 16384)):
    P, sb, S, ps = C.P, C.sb, C.sem, C.ps
    big = sb["BIG"]
    P.y.wait(S["lds"])
    P.y.wait(S["pe"])
    P.y.wait(S["dv"])
    P.y.wait(S["st"])
    P.y.dma_start(out=big[:, 0:kv_cols[0]], in_=C.scr_kv[:, 0:kv_cols[0]]
                  ).then_inc(S["ldx"], 16)
    P.y.dma_start(out=big[:, 16384:16384 + kv_cols[1]],
                  in_=C.scr_kv[:, 16384:16384 + kv_cols[1]]).then_inc(S["ldx"], 16)
    et_of = [big[:, 4096:7040], big[:, 9984:12928]]
    maskz = big[:, 7040:9984]
    PT = [40960, 49152]
    et_mark = [0, 0]
    et_ready = [0, 0]
    if alibi is not None:
        P.y.dma_start(out=maskz, in_=C.dram["c_bmaskz"]).then_inc(S["ldx"], 16)
        P.y.dma_start(out=sb["BF32"][:, 0:2944], in_=C.dram["c_balibi"]
                      ).then_inc(S["ldx"], 16)
    ldx_kv = S["ldx"].n
    qh_mark = [0, 0]
    astg_mark = [0, 0]
    probs_mark = [[0] * 16, [0] * 16]
    bank_gate = {}
    g0 = (S["dv"], S["dv"].n)
    sums_mark = [0]
    av_mark = [0]
    b3_gate = [(S["ac"], 0)]
    rb_rd = [0, 0]
    pe_sums = [0, 0]

    def pt_tile(sub, kc, w=512):
        return big[:, PT[sub] + kc * 512:PT[sub] + kc * 512 + w]

    def emit_et(h):
        sub = h % 2
        et = et_of[sub]
        P.s.wait(S["ldx"], ldx_kv)
        P.s.wait(S["pl"], et_mark[sub])
        P.s.activation(out=et, in_=sb["BF32"][:, 0:2944], func=AF.Exp,
                       scale=np.float32(alibi[h])).then_inc(S["ac"], 1)
        P.g.wait(S["ac"], S["ac"].n)
        P.g.tensor_tensor(out=et, in0=et, in1=maskz, op=ALU.mult
                          ).then_inc(S["pl"], 1)
        et_ready[sub] = S["pl"].n

    def s1(h, qt, qv):
        sub = h % 2
        kvh = kv_of(h)
        et = et_of[sub]
        qh = big[:, 32768 + sub * 2048:32768 + (sub + 1) * 2048]
        nkc = 4 * (qt + 1)
        for kc in range(nkc):
            b = kc % 3
            gate = bank_gate.get(b, g0)
            if gate[1] > 0:
                P.t.wait(*gate)
            if h < 2 and qt == 0 and kc == 0:
                P.t.wait(S["ldx"], ldx_kv)
            if qt == 0 and kc == 0:
                P.t.wait(S["ldx"], qv)
            P.t.matmul(ps[b][:, :],
                       big[:, kr_cols(kvh, kc):kr_cols(kvh, kc) + 128],
                       qh[:, qt * 512:(qt + 1) * 512],
                       start=True, stop=True).then_inc(S["pe"], 1)
            P.s.wait(S["pe"])
            P.s.wait(S["pe"], probs_mark[sub][kc])
            P.s.activation(out=pt_tile(sub, kc), in_=ps[b][:, :],
                           func=AF.Exp, scale=np.float32(scale)
                           ).then_inc(S["ac"], 1)
            bank_gate[b] = (S["ac"], S["ac"].n)
            if alibi is not None:
                off = (qt * 512 - kc * 128) + 384
                eng, sem = (P.g, "pl") if kc % 4 == 3 else (P.v, "dv")
                eng.wait(S["ac"], S["ac"].n)
                eng.wait(S["pl"], et_ready[sub])
                eng.tensor_tensor(out=pt_tile(sub, kc), in0=pt_tile(sub, kc),
                                  in1=et[:, off:off + 512],
                                  op=ALU.mult).then_inc(S[sem], 1)
            elif kc * 128 >= qt * 512:
                off = (qt * 512 - kc * 128) + 384
                eng, sem = (P.g, "pl") if kc % 2 == 1 else (P.v, "dv")
                eng.wait(S["ac"], S["ac"].n)
                eng.tensor_tensor(out=pt_tile(sub, kc), in0=pt_tile(sub, kc),
                                  in1=C.mask896[:, off:off + 512],
                                  op=ALU.mult).then_inc(S[sem], 1)
        return (S["ac"].n, S["pl"].n, S["dv"].n)

    def s2(h, qt, marks):
        sub = h % 2
        kvh = kv_of(h)
        nkc = 4 * (qt + 1)
        acm, plm, dvm = marks
        P.t.wait(S["ac"], acm)
        if plm > 0:
            P.t.wait(S["pl"], plm)
        P.t.wait(S["dv"], dvm)
        P.t.wait(S["dv"], sums_mark[0])
        for kc in range(nkc):
            mm = P.t.matmul(ps[5][0:1, :], C.ones128, pt_tile(sub, kc),
                            start=(kc == 0), stop=(kc == nkc - 1))
        mm.then_inc(S["pe"], 1)
        pe_sums[sub] = S["pe"].n
        P.t.wait(S["dv"], av_mark[0])
        for kc in range(nkc):
            mm = P.t.matmul(ps[4][:, :],
                            big[:, vt_cols(kvh, kc):vt_cols(kvh, kc) + 128],
                            pt_tile(sub, kc),
                            start=(kc == 0), stop=(kc == nkc - 1))
        mm.then_inc(S["pe"], 1)
        pe_av = S["pe"].n
        for kc in range(nkc):
            probs_mark[sub][kc] = pe_av
        r_sum = sb["ROWS"][:, 512 + sub * 512:1024 + sub * 512]
        P.v.wait(S["pe"], pe_sums[sub])
        P.v.reciprocal(out=r_sum, in_=ps[5][0:1, :]).then_inc(S["dv"], 1)
        sums_mark[0] = S["dv"].n

    def s3(h, qt):
        sub = h % 2
        astg = big[:, 36864 + sub * 2048:36864 + (sub + 1) * 2048]
        r_sum = sb["ROWS"][:, 512 + sub * 512:1024 + sub * 512]
        rb = sb["BF32"][:, 4096 + sub * 512:4608 + sub * 512]
        if b3_gate[0][1] > 0:
            P.t.wait(*b3_gate[0])
        P.t.wait(S["dv"], sums_mark[0])
        P.t.matmul(ps[3][:, :], C.onesrow_f[:, 0:128], r_sum,
                   start=True, stop=True).then_inc(S["pe"], 1)
        P.s.wait(S["pe"])
        P.s.wait(S["dv"], rb_rd[sub])
        P.s.copy(out=rb, in_=ps[3][:, :]).then_inc(S["ac"], 1)
        b3_gate[0] = (S["ac"], S["ac"].n)
        P.v.wait(S["ac"])
        if qt == 0:
            P.v.wait(S["lds"], astg_mark[sub])
        P.v.tensor_tensor(out=astg[:, qt * 512:(qt + 1) * 512],
                          in0=ps[4][:, :], in1=rb, op=ALU.mult
                          ).then_inc(S["dv"], 1)
        av_mark[0] = S["dv"].n
        rb_rd[sub] = S["dv"].n

    for hp in range(nheads // 2):
        qv = [0, 0]
        for sub in range(2):
            h = 2 * hp + sub
            qh = big[:, 32768 + sub * 2048:32768 + (sub + 1) * 2048]
            P.y.wait(S["pe"], qh_mark[sub])
            P.y.dma_start(out=qh, in_=C.scr_q[:, h * 2048:(h + 1) * 2048]
                          ).then_inc(S["ldx"], 16)
            qv[sub] = S["ldx"].n
            if alibi is not None:
                emit_et(h)
        for qt in range(NTT):
            mA = s1(2 * hp, qt, qv[0])
            mB = s1(2 * hp + 1, qt, qv[1])
            s2(2 * hp, qt, mA)
            s3(2 * hp, qt)
            s2(2 * hp + 1, qt, mB)
            s3(2 * hp + 1, qt)
        for sub in range(2):
            h = 2 * hp + sub
            astg = big[:, 36864 + sub * 2048:36864 + (sub + 1) * 2048]
            P.s.wait(S["dv"])
            P.s.dma_start(out=C.scr_attn[:, h * 2048:(h + 1) * 2048], in_=astg
                          ).then_inc(S["lds"], 16)
            astg_mark[sub] = S["lds"].n
        qh_mark[0] = S["pe"].n
        qh_mark[1] = S["pe"].n
        et_mark[0] = S["pl"].n
        et_mark[1] = S["pl"].n


def epi_store(C, scr_dst_fn):
    P, sb, S = C.P, C.sb, C.sem
    asm_mark = [0]

    def f(oc, tt, psum):
        asm = sb["MED"][:, 6144 + tt * 512:6144 + (tt + 1) * 512]
        P.s.wait(S["pe"])
        if tt == 0:
            P.s.wait(S["lds"], asm_mark[0])
        P.s.activation(out=asm, in_=psum, func=AF.Copy).then_inc(S["ac"], 1)
        return (S["ac"], S["ac"].n)

    def flush(oc):
        P.s.dma_start(out=scr_dst_fn(oc), in_=sb["MED"][:, 6144:8192]
                      ).then_inc(S["lds"], 16)
        asm_mark[0] = S["lds"].n
        C.med_lds = asm_mark[0]
    f.flush = flush
    return f


def alibi_slopes(n=32):
    import math
    cp2 = 2 ** int(math.floor(math.log2(n)))
    base = 2.0 ** (-(2.0 ** (-(math.log2(cp2) - 3))))
    slopes = [base ** i for i in range(1, cp2 + 1)]
    if cp2 != n:
        extra_base = 2.0 ** (-(2.0 ** (-(math.log2(2 * cp2) - 3))))
        rem = min(cp2, n - cp2)
        slopes += [extra_base ** i for i in range(1, 1 + 2 * rem, 2)]
    return slopes


def emit_l1_attn_b(C, final=False):
    P, sb, S = C.P, C.sb, C.sem
    pre = preload_w(C, "l1b_kblk", 4096, 1)
    ln_into_big(C, C.dram["xT"])
    proj_loop(C, "l1b_kblk", None, 1, 4096,
              terms_plain(C, lambda dc, t0, t1: xn_ap(C, dc, t0, t1)),
              epi_store(C, lambda oc: C.scr_kv[:, oc * 2048:(oc + 1) * 2048]),
              tt_gates=C.ln_tt_dv, pre=pre)
    emit_v_tm(C, "l1b_vblk", 1, 16384)
    proj_loop(C, "l1b_qblk", None, NCH, 4096,
              terms_plain(C, lambda dc, t0, t1: xn_ap(C, dc, t0, t1)),
              epi_store(C, lambda oc: C.scr_q[:, oc * 2048:(oc + 1) * 2048]))
    emit_attn_core(C, nheads=32, kv_of=lambda h: 0,
                   kr_cols=lambda kvh, kc: kc * 128,
                   vt_cols=lambda kvh, kc: 16384 + kc * 128,
                   scale=1.0 / np.sqrt(128.0), kind="softmax",
                   alibi=alibi_slopes(32), kv_cols=(2048, 2048))
    _wo_pass(C, "l1b_oblk", final)


def ln_stats_only(C, x_src):
    P, sb, S, ps = C.P, C.sb, C.sem, C.ps
    xr = x_src.rearrange("(c p) t -> p c t", p=128)
    P.y.wait(S["st"])
    dv_mark = [S["dv"].n, S["dv"].n]
    pe_mark = [S["pe"].n, S["pe"].n]
    pe_entry = S["pe"].n
    for tt in range(NTT):
        P.t.wait(S["dv"])
        for cb in range(16):
            sl = cb % 2
            buf = sb["BF32"][:, 4096 + sl * 1024:4096 + (sl + 1) * 1024]
            xb = sb["MED"][:, sl * 1024:(sl + 1) * 1024]
            x2 = sb["MED"][:, 2048 + sl * 1024:2048 + (sl + 1) * 1024]
            P.y.wait(S["dv"], dv_mark[sl])
            P.y.dma_start(out=buf,
                          in_=xr[:, 2 * cb:2 * cb + 2, tt * TT:(tt + 1) * TT]
                          ).then_inc(S["ldx"], 16)
            P.v.wait(S["ldx"])
            P.v.wait(S["pe"], pe_mark[sl])
            P.v.tensor_copy(out=xb, in_=buf).then_inc(S["dv"], 1)
            P.s.wait(S["dv"])
            P.s.activation(out=x2, in_=xb, func=AF.Square).then_inc(S["ac"], 1)
            P.t.wait(S["dv"])
            first = (cb == 0)
            for k in range(2):
                P.t.matmul(ps[4][0:1, :], C.ones128, xb[:, k * TT:(k + 1) * TT],
                           start=(first and k == 0), stop=(cb == 15 and k == 1))
            P.t.wait(S["ac"])
            for k in range(2):
                mm = P.t.matmul(ps[5][0:1, :], C.ones128, x2[:, k * TT:(k + 1) * TT],
                                start=(first and k == 0), stop=(cb == 15 and k == 1))
            mm.then_inc(S["pe"], 1)
            pe_mark[sl] = S["pe"].n
            dv_mark[sl] = S["dv"].n
        r_mean = sb["ROWS"][:, 512:1024]
        r_var = sb["ROWS"][:, 1024:1536]
        r_m2 = sb["ROWS"][:, 1536:2048]
        r_rstd = sb["ROWS"][:, 2048:2560]
        r_mr = sb["ROWS"][:, 2560:3072]
        P.v.wait(S["pe"])
        P.v.tensor_scalar(out=r_mean, in0=ps[4][0:1, :], scalar1=1.0 / D,
                          scalar2=None, op0=ALU.mult).then_inc(S["dv"], 1)
        P.v.tensor_scalar(out=r_var, in0=ps[5][0:1, :], scalar1=1.0 / D,
                          scalar2=None, op0=ALU.mult).then_inc(S["dv"], 1)
        P.s.wait(S["dv"])
        P.s.activation(out=r_m2, in_=r_mean, func=AF.Square).then_inc(S["ac"], 1)
        P.v.wait(S["ac"])
        P.v.tensor_tensor(out=r_var, in0=r_var, in1=r_m2, op=ALU.subtract
                          ).then_inc(S["dv"], 1)
        P.s.wait(S["dv"])
        P.s.activation(out=r_var, in_=r_var, func=AF.Ln, bias=np.float32(1e-5)
                       ).then_inc(S["ac"], 1)
        P.s.activation(out=r_rstd, in_=r_var, func=AF.Exp, scale=np.float32(-0.5)
                       ).then_inc(S["ac"], 1)
        P.v.wait(S["ac"])
        P.v.tensor_tensor(out=r_mr, in0=r_mean, in1=r_rstd, op=ALU.mult
                          ).then_inc(S["dv"], 1)
        P.t.wait(S["dv"])
        P.t.matmul(ps[4][:, :], C.onesrow_f[:, 0:128], r_rstd, start=True, stop=True)
        mm = P.t.matmul(ps[5][:, :], C.onesrow_f[:, 0:128], r_mr, start=True, stop=True)
        mm.then_inc(S["pe"], 1)
        P.v.wait(S["pe"])
        if tt == 0:
            P.v.wait(S["pe"], pe_entry)
            P.v.wait(S["lds"])
        P.v.tensor_copy(out=sb["BF32"][:, tt * TT:(tt + 1) * TT], in_=ps[4][:, :]
                        ).then_inc(S["dv"], 1)
        P.v.tensor_copy(out=sb["BF32"][:, 2048 + tt * TT:2048 + (tt + 1) * TT],
                        in_=ps[5][:, :]).then_inc(S["dv"], 1)


def emit_bind_prep(C):
    P, sb, S, ps = C.P, C.sb, C.sem, C.ps
    B32 = sb["BF32"]
    P.y.wait(S["dv"])
    P.y.wait(S["pe"])
    P.y.wait(S["ac"])
    for i, nm in enumerate(["l0a_bvq", "l0a_bvk", "l0a_bvv"]):
        src_ap = C.dram[nm].flatten().rearrange("(c p) -> p c", p=128)
        P.y.dma_start(out=B32[:, i * 32:(i + 1) * 32], in_=src_ap,
                      allow_slow_non_contiguous=True).then_inc(S["ldx"], 16)
    P.y.dma_start(out=B32[0:96, 3168:3264], in_=C.dram["c_ghead"]
                  ).then_inc(S["ldx"], 16)
    P.v.wait(S["ldx"])
    P.v.memset(B32[:, 96:512], 0.0).then_inc(S["dv"], 1)
    bvs = B32[:, 0:512]
    sgn = B32[:, 512:1024]
    s2q = B32[:, 1024:1536]
    ab = B32[:, 1536:2048]
    colsb = B32[0:96, 2048:2176]
    alb_sb = B32[:, 2176:2272]
    s2b = B32[:, 2560:3072]
    G = B32[:, 3072:3168]
    abb = sb["MED"][:, 0:512]
    r1 = sb["ROWS"][:, 512:608]
    P.s.wait(S["ldx"])
    P.s.wait(S["dv"])
    P.s.activation(out=sgn, in_=bvs, func=AF.Sign).then_inc(S["ac"], 1)
    P.s.activation(out=s2q, in_=sgn, func=AF.Square).then_inc(S["ac"], 1)
    P.v.wait(S["ac"])
    P.v.tensor_tensor(out=ab, in0=bvs, in1=sgn, op=ALU.mult).then_inc(S["dv"], 1)
    P.v.tensor_copy(out=abb, in_=ab).then_inc(S["dv"], 1)
    P.t.wait(S["dv"])
    P.t.matmul(ps[4][0:1, 0:96], C.ones128, abb[:, 0:96], start=True, stop=True
               ).then_inc(S["pe"], 1)
    P.v.wait(S["pe"])
    P.v.tensor_copy(out=r1, in_=ps[4][0:1, 0:96]).then_inc(S["dv"], 1)
    P.s.wait(S["dv"])
    P.s.dma_start(out=C.scr_misc[0:1, 8192:8288], in_=r1).then_inc(S["lds"], 16)
    P.y.wait(S["lds"])
    P.y.dma_start(out=B32[0:96, 2048:2049],
                  in_=C.scr_misc[0:1, 8192:8288].rearrange("o (p c) -> (o p) c", c=1)
                  ).then_inc(S["ldx"], 16)
    P.v.wait(S["ldx"])
    P.v.tensor_copy(out=colsb, in_=B32[0:96, 2048:2049].broadcast_to((96, 128))
                    ).then_inc(S["dv"], 1)
    P.t.wait(S["dv"])
    P.t.matmul(ps[5][:, 0:96], colsb, B32[0:96, 3168:3264], start=True, stop=True
               ).then_inc(S["pe"], 1)
    P.s.wait(S["pe"])
    P.s.copy(out=alb_sb, in_=ps[5][:, 0:96]).then_inc(S["ac"], 1)
    P.v.tensor_tensor(out=s2b, in0=sgn, in1=s2q, op=ALU.subtract).then_inc(S["dv"], 1)
    P.s.wait(S["dv"])
    P.s.add(out=s2b, in_=s2b, add=1.0).then_inc(S["ac"], 1)
    P.v.wait(S["ac"])
    P.v.tensor_tensor(out=G, in0=s2b[:, 0:96], in1=alb_sb, op=ALU.mult
                      ).then_inc(S["dv"], 1)
    P.v.tensor_copy(out=B32[:, 3584:4096], in_=B32[:, 1024:1536]
                    ).then_inc(S["dv"], 1)
    P.v.tensor_tensor(out=sb["SF32"][:, 388:420], in0=G[:, 0:32], in1=G[:, 32:64],
                      op=ALU.mult).then_inc(S["dv"], 1)
    P.v.tensor_copy(out=B32[:, 3584:4096], in_=B32[:, 1024:1536]
                    ).then_inc(S["dv"], 1)
    P.v.tensor_copy(out=sb["SF32"][:, 420:452], in_=G[:, 64:96]).then_inc(S["dv"], 1)
    if getattr(C, "dump", None) == "scr_misc":
        P.s.wait(S["dv"])
        P.s.dma_start(out=C.scr_misc[0:1, 0:4096].rearrange("o (c p) -> (o p) c", p=128),
                      in_=sb["SF32"][:, 388:420],
                      allow_slow_non_contiguous=True).then_inc(S["lds"], 16)
        P.s.dma_start(out=C.scr_misc[0:1, 4096:8192].rearrange("o (c p) -> (o p) c", p=128),
                      in_=sb["SF32"][:, 420:452],
                      allow_slow_non_contiguous=True).then_inc(S["lds"], 16)


def emit_l0_attn_a(C, final=False):
    P, sb, S, ps = C.P, C.sb, C.sem, C.ps
    big = sb["BIG"]
    med = sb["MED"]
    emit_bind_prep(C)
    ln_stats_only(C, C.dram["xT"])
    rb = sb["BF32"][:, 0:2048]
    mrb = sb["BF32"][:, 2048:4096]
    xr = C.dram["xT"].rearrange("(c p) t -> p c t", p=128)
    probs_mark = [0] * 16
    bank_gate = {}
    g0 = (S["dv"], S["dv"].n)
    tr_mark = [S["ac"].n, S["ac"].n]
    av_mark = [S["dv"].n, S["dv"].n]
    aout_mark = [0]
    scores_end = [0]
    xn_pe_free = [0]
    xld_of = {}

    def emit_load_x(h):
        bx = 16384 * (h % 2)
        xbig = big[:, bx:bx + 16384].rearrange("p (c t) -> p c t", t=T)
        P.g.wait(S["pe"], scores_end[0])
        if h == 0:
            P.g.wait(S["dv"])
        xld = [0, 0]
        for hf in range(2):
            P.g.dma_start(out=xbig[:, 4 * hf:4 * hf + 4, :],
                          in_=xr[:, 8 * h + 4 * hf:8 * h + 4 * hf + 4, :]
                          ).then_inc(S["ldx"], 16)
            xld[hf] = S["ldx"].n
        xld_of[h] = xld

    emit_load_x(0)
    for h in range(4):
        bx = 16384 * (h % 2)
        bk = 16384 - bx
        xld = xld_of[h]
        for c8 in range(8):
            dc = 8 * h + c8
            t1 = sb["BF32"][:, 4096:6144]
            xs = big[:, bx + c8 * 2048:bx + (c8 + 1) * 2048]
            P.v.wait(S["ldx"], xld[c8 // 4])
            P.v.wait(S["pe"], xn_pe_free[0])
            P.v.tensor_tensor(out=t1, in0=xs, in1=rb, op=ALU.mult
                              ).then_inc(S["dv"], 1)
            P.v.tensor_tensor(out=xs, in0=t1, in1=mrb, op=ALU.subtract
                              ).then_inc(S["dv"], 1)
            P.v.tensor_scalar(out=big[:, bk + c8 * 2048:bk + (c8 + 1) * 2048],
                              in0=xs,
                              scalar1=sb["SF32"][:, 388 + dc:389 + dc],
                              scalar2=None, op0=ALU.mult).then_inc(S["dv"], 1)
        prep_dv = S["dv"].n
        for c8 in range(8):
            P.t.wait(S["dv"], prep_dv)
            for tc in range(16):
                pb = ps[6 + (tc % 2)]
                P.t.wait(S["ac"], tr_mark[tc % 2])
                P.t.transpose(pb[:, 0:128],
                              big[:, bx + c8 * 2048 + tc * 128:
                                   bx + c8 * 2048 + (tc + 1) * 128],
                              C.ident).then_inc(S["pe"], 1)
                P.s.wait(S["pe"])
                P.s.copy(out=big[:, 32768 + tc * 1024 + c8 * 128:
                                 32768 + tc * 1024 + (c8 + 1) * 128],
                         in_=pb[:, 0:128]).then_inc(S["ac"], 1)
                tr_mark[tc % 2] = S["ac"].n
        for qt in range(NTT):
            nkc = 4 * (qt + 1)
            for kc in range(nkc):
                b = kc % 4
                gate = bank_gate.get(b, g0)
                if gate[1] > 0:
                    P.t.wait(*gate)
                if kc == 0:
                    P.t.wait(S["dv"], prep_dv)
                for c8 in range(8):
                    mm = P.t.matmul(ps[b][:, :],
                                    big[:, bk + c8 * 2048 + kc * 128:
                                         bk + c8 * 2048 + (kc + 1) * 128],
                                    big[:, bx + c8 * 2048 + qt * 512:
                                         bx + c8 * 2048 + (qt + 1) * 512],
                                    start=(c8 == 0), stop=(c8 == 7))
                mm.then_inc(S["pe"], 1)
                P.s.wait(S["pe"])
                P.s.wait(S["pe"], probs_mark[kc])
                P.s.activation(out=med[:, kc * 512:(kc + 1) * 512], in_=ps[b][:, :],
                               func=AF.Sigmoid, scale=np.float32(0.125)
                               ).then_inc(S["ac"], 1)
                bank_gate[b] = (S["ac"], S["ac"].n)
                if kc * 128 >= qt * 512:
                    off = (qt * 512 - kc * 128) + 384
                    P.v.wait(S["ac"])
                    P.v.tensor_tensor(out=med[:, kc * 512:(kc + 1) * 512],
                                      in0=med[:, kc * 512:(kc + 1) * 512],
                                      in1=C.mask896[:, off:off + 512],
                                      op=ALU.mult).then_inc(S["dv"], 1)
            if qt == NTT - 1:
                scores_end[0] = S["pe"].n
            P.t.wait(S["ac"])
            P.t.wait(S["dv"])
            for c8 in range(8):
                bb = 4 + (c8 % 2)
                P.t.wait(S["dv"], av_mark[c8 % 2])
                for kc in range(nkc):
                    mm = P.t.matmul(ps[bb][:, :],
                                    big[:, 32768 + kc * 1024 + c8 * 128:
                                         32768 + kc * 1024 + (c8 + 1) * 128],
                                    med[:, kc * 512:(kc + 1) * 512],
                                    start=(kc == 0), stop=(kc == nkc - 1))
                mm.then_inc(S["pe"], 1)
                dco = 8 * h + c8
                P.v.wait(S["pe"])
                if qt == 0 and c8 == 0:
                    P.v.wait(S["st"], aout_mark[0])
                P.v.tensor_scalar(out=big[:, 49152 + c8 * 2048 + qt * 512:
                                          49152 + c8 * 2048 + (qt + 1) * 512],
                                  in0=ps[bb][:, :],
                                  scalar1=sb["SF32"][:, 420 + dco:421 + dco],
                                  scalar2=None, op0=ALU.mult).then_inc(S["dv"], 1)
                av_mark[c8 % 2] = S["dv"].n
            for kc in range(nkc):
                probs_mark[kc] = S["pe"].n
        xn_pe_free[0] = scores_end[0]
        if h + 1 < 4:
            emit_load_x(h + 1)
        P.g.wait(S["dv"])
        for c8 in range(8):
            dc = 8 * h + c8
            P.g.dma_start(out=C.dram["xT"][dc * 128:(dc + 1) * 128, :],
                          in_=big[:, 49152 + c8 * 2048:49152 + (c8 + 1) * 2048],
                          accum_op=ALU.add).then_inc(S["st"], 16)
        aout_mark[0] = S["st"].n
        P.g.wait(S["st"], S["st"].n)
        P.g.dma_start(out=C.scr_xn[:, 8 * h * T:(8 * h + 8) * T].rearrange(
                          "p (c t) -> p c t", t=T),
                      in_=xr[:, 8 * h:8 * h + 8, :]).then_inc(S["lds"], 16)
        for c8 in range(8):
            C.xbf_marks[8 * h + c8] = S["lds"].n
    if final:
        P.y.wait(S["st"])
        P.y.dma_start(out=C.out, in_=C.dram["xT"]).then_inc(S["ldx"], 16)


def emit_bind_only(C, final=False):
    emit_bind_prep(C)
    if final:
        P, S = C.P, C.sem
        P.y.wait(S["lds"])
        P.y.dma_start(out=C.out, in_=C.dram["xT"]).then_inc(S["ldx"], 16)


def emit_l0b_windows(C):
    P, sb, S, ps = C.P, C.sb, C.sem, C.ps
    big = sb["BIG"]
    med = sb["MED"]
    P.y.wait(S["lds"])
    P.y.wait(S["pe"])
    P.y.wait(S["dv"])
    P.y.dma_start(out=big[:, 0:2048], in_=C.scr_kv[:, 0:2048]).then_inc(S["ldx"], 16)
    P.y.dma_start(out=big[:, 16384:18432], in_=C.scr_kv[:, 16384:18432]
                  ).then_inc(S["ldx"], 16)
    P.y.dma_start(out=sb["ROWS"][:, 512:1024], in_=C.dram["c_gmid"]).then_inc(S["ldx"], 16)
    P.y.dma_start(out=sb["ROWS"][:, 1024:1536], in_=C.dram["c_gw0"]).then_inc(S["ldx"], 16)
    P.y.dma_start(out=sb["ROWS"][:, 1536:1792], in_=C.dram["c_gw7"]).then_inc(S["ldx"], 16)
    ldx_kv = S["ldx"].n
    gb = {"mid": sb["BF32"][:, 0:512], "w0": sb["BF32"][:, 512:1024],
          "w7": sb["BF32"][:, 1024:1280]}
    P.t.wait(S["ldx"], ldx_kv)
    P.t.wait(S["dv"])
    P.t.wait(S["ac"])
    P.t.matmul(ps[3][:, :], C.onesrow_f[:, 0:128], sb["ROWS"][:, 512:1024],
               start=True, stop=True).then_inc(S["pe"], 1)
    P.s.wait(S["pe"])
    P.s.copy(out=gb["mid"], in_=ps[3][:, :]).then_inc(S["ac"], 1)
    P.t.wait(S["ac"])
    P.t.matmul(ps[3][:, :], C.onesrow_f[:, 0:128], sb["ROWS"][:, 1024:1536],
               start=True, stop=True).then_inc(S["pe"], 1)
    P.s.wait(S["pe"])
    P.s.copy(out=gb["w0"], in_=ps[3][:, :]).then_inc(S["ac"], 1)
    P.t.wait(S["ac"])
    P.t.matmul(ps[3][:, 0:256], C.onesrow_f[:, 0:128], sb["ROWS"][:, 1536:1792],
               start=True, stop=True).then_inc(S["pe"], 1)
    P.s.wait(S["pe"])
    P.s.copy(out=gb["w7"], in_=ps[3][:, 0:256]).then_inc(S["ac"], 1)
    qh_mark = [0, 0, 0, 0]
    astg_mark = [0, 0]
    probs_mark = [[0] * 4, [0] * 4]
    bank_gate = [0, 0]
    b5_gate = [0, 0]
    sums_mark = [0, 0]
    av_mark = [0, 0]
    rb_rd = [0, 0]
    dst_pl = [0, 0]
    qv = [0, 0]
    pe_sums = [0, 0]
    sc = [0]

    def qh_of(h):
        return big[:, 32768 + (h % 4) * 2048:32768 + (h % 4 + 1) * 2048]

    def s1(h, w):
        sub = h % 2
        qh = qh_of(h)
        q0 = 256 * w
        qw = 512 if w < 7 else 256
        for kc in range(qw // 128):
            b = sc[0] % 2
            sc[0] += 1
            if bank_gate[b] > 0:
                P.t.wait(S["ac"], bank_gate[b])
            if w == 0 and kc == 0:
                P.t.wait(S["ldx"], qv[sub])
            P.t.matmul(ps[b][:, 0:qw],
                       big[:, q0 + kc * 128:q0 + (kc + 1) * 128],
                       qh[:, q0:q0 + qw],
                       start=True, stop=True).then_inc(S["pe"], 1)
            P.s.wait(S["pe"])
            P.s.wait(S["pe"], probs_mark[sub][kc])
            P.s.activation(out=med[:, sub * 2048 + kc * 512:sub * 2048 + kc * 512 + qw],
                           in_=ps[b][:, 0:qw],
                           func=AF.Sigmoid, scale=np.float32(1.0 / np.sqrt(128.0))
                           ).then_inc(S["ac"], 1)
            bank_gate[b] = S["ac"].n
        return S["ac"].n

    def s2(h, w, sig_ac):
        sub = h % 2
        ps_s = ps[5] if sub == 0 else ps[2]
        ps_av = ps[4] if sub == 0 else ps[3]
        q0 = 256 * w
        qw = 512 if w < 7 else 256
        nkc = qw // 128
        P.t.wait(S["ac"], sig_ac)
        if b5_gate[sub] > 0:
            P.t.wait(S["ac"], b5_gate[sub])
        for kc in range(nkc):
            mm = P.t.matmul(ps_s[0:1, 0:qw], C.ones128,
                            med[:, sub * 2048 + kc * 512:sub * 2048 + kc * 512 + qw],
                            start=(kc == 0), stop=(kc == nkc - 1))
        mm.then_inc(S["pe"], 1)
        pe_sums[sub] = S["pe"].n
        if av_mark[sub] > 0:
            P.t.wait(S["dv"], av_mark[sub])
        for kc in range(nkc):
            mm = P.t.matmul(ps_av[:, 0:qw],
                            big[:, 16384 + (2 * w + kc) * 128:16384 + (2 * w + kc + 1) * 128],
                            med[:, sub * 2048 + kc * 512:sub * 2048 + kc * 512 + qw],
                            start=(kc == 0), stop=(kc == nkc - 1))
        mm.then_inc(S["pe"], 1)
        pe_av = S["pe"].n
        for kc in range(nkc):
            probs_mark[sub][kc] = pe_av
        r_sum = sb["ROWS"][:, 1792 + sub * 512:1792 + sub * 512 + qw]
        P.v.wait(S["pe"], pe_sums[sub])
        P.v.reciprocal(out=r_sum, in_=ps_s[0:1, 0:qw]).then_inc(S["dv"], 1)
        sums_mark[sub] = S["dv"].n

    def s3(h, w):
        sub = h % 2
        ps_s = ps[5] if sub == 0 else ps[2]
        ps_av = ps[4] if sub == 0 else ps[3]
        q0 = 256 * w
        qw = 512 if w < 7 else 256
        prof = gb["w0"] if w == 0 else (gb["w7"] if w == 7 else gb["mid"])
        r_sum = sb["ROWS"][:, 1792 + sub * 512:1792 + sub * 512 + qw]
        rb_sb = sb["BF32"][:, 1536 + sub * 512:1536 + sub * 512 + qw]
        t_sb = sb["BF32"][:, 2560 + sub * 512:2560 + sub * 512 + qw]
        E = big[:, 40960 + sub * 4096:40960 + sub * 4096 + 2048]
        O = big[:, 40960 + sub * 4096 + 2048:40960 + sub * 4096 + 4096]
        P.t.wait(S["dv"], sums_mark[sub])
        P.t.matmul(ps_s[:, 0:qw], C.onesrow_f[:, 0:128], r_sum,
                   start=True, stop=True).then_inc(S["pe"], 1)
        P.s.wait(S["pe"])
        P.s.wait(S["dv"], rb_rd[sub])
        P.s.copy(out=rb_sb, in_=ps_s[:, 0:qw]).then_inc(S["ac"], 1)
        b5_gate[sub] = S["ac"].n
        dst = (E if w % 2 == 0 else O)[:, q0:q0 + qw]
        P.v.wait(S["ac"])
        if dst_pl[sub] > 0:
            P.v.wait(S["pl"], dst_pl[sub])
        P.v.tensor_tensor(out=t_sb, in0=ps_av[:, 0:qw], in1=rb_sb,
                          op=ALU.mult).then_inc(S["dv"], 1)
        av_mark[sub] = S["dv"].n
        rb_rd[sub] = S["dv"].n
        P.g.wait(S["dv"], S["dv"].n)
        P.g.tensor_tensor(out=dst, in0=t_sb, in1=prof[:, 0:qw],
                          op=ALU.mult).then_inc(S["pl"], 1)
        dst_pl[sub] = S["pl"].n

    pe_entry_w = S["pe"].n
    for hp in range(16):
        for sub in range(2):
            h = 2 * hp + sub
            O = big[:, 40960 + sub * 4096 + 2048:40960 + sub * 4096 + 4096]
            P.y.wait(S["pe"], qh_mark[h % 4])
            P.y.dma_start(out=qh_of(h), in_=C.scr_q[:, h * 2048:(h + 1) * 2048]
                          ).then_inc(S["ldx"], 16)
            qv[sub] = S["ldx"].n
            if hp == 0:
                P.v.wait(S["pe"], pe_entry_w)
            P.v.memset(O[:, 0:256], 0.0).then_inc(S["dv"], 1)
        for w in range(8):
            hA, hB = 2 * hp, 2 * hp + 1
            acA = s1(hA, w)
            acB = s1(hB, w)
            s2(hA, w, acA)
            s2(hB, w, acB)
            s3(hA, w)
            s3(hB, w)
        for sub in range(2):
            h = 2 * hp + sub
            astg = big[:, 2048 + sub * 2048:2048 + (sub + 1) * 2048]
            E = big[:, 40960 + sub * 4096:40960 + sub * 4096 + 2048]
            O = big[:, 40960 + sub * 4096 + 2048:40960 + sub * 4096 + 4096]
            P.v.wait(S["lds"], astg_mark[sub])
            P.v.wait(S["pl"], dst_pl[sub])
            P.v.tensor_tensor(out=astg, in0=E, in1=O, op=ALU.add).then_inc(S["dv"], 1)
            P.s.wait(S["dv"])
            P.s.dma_start(out=C.scr_attn[:, h * 2048:(h + 1) * 2048], in_=astg
                          ).then_inc(S["lds"], 16)
            astg_mark[sub] = S["lds"].n
        qh_mark[(2 * hp) % 4] = S["pe"].n
        qh_mark[(2 * hp + 1) % 4] = S["pe"].n


def emit_l0_attn_b(C, final=False):
    pre = preload_w(C, "l0b_kblk", 4096, 1)
    load_rope_tables(C, 0)
    ln_into_big(C, C.dram["xT"])
    proj_loop(C, "l0b_kblk", None, 1, 4096,
              terms_plain(C, lambda dc, t0, t1: xn_ap(C, dc, t0, t1)),
              epi_rope_store(C, lambda oc: C.scr_kv[:, oc * 2048:(oc + 1) * 2048]),
              tt_gates=C.ln_tt_dv, pre=pre)
    emit_v_tm(C, "l0b_vblk", 1, 16384)
    proj_loop(C, "l0b_qblk", None, NCH, 4096,
              terms_plain(C, lambda dc, t0, t1: xn_ap(C, dc, t0, t1)),
              epi_rope_store(C, lambda oc: C.scr_q[:, oc * 2048:(oc + 1) * 2048]))
    emit_l0b_windows(C)
    _wo_pass(C, "l0b_oblk", final)


EMITTERS = {
    "bind_only": emit_bind_only,
    "l0_attn_b": emit_l0_attn_b,
    "l0_attn_a": emit_l0_attn_a,
    "l0_ffn": emit_l0_ffn,
    "l1_ffn": emit_l1_ffn,
    "l1_attn_a": emit_l1_attn_a,
    "l1_attn_b": emit_l1_attn_b,
}

SUBLAYERS_ALL = ["l0_attn_a", "l0_ffn", "l0_attn_b", "l0_ffn",
                 "l1_attn_a", "l1_ffn", "l1_attn_b", "l1_ffn"]



def host_constants():
    c = {}
    c["c_ident"] = np.eye(128, dtype=BF)
    P = np.zeros((128, 128), dtype=np.float32)
    for i in range(64):
        P[64 + i, i] = -1.0
        P[i, 64 + i] = 1.0
    c["c_prot"] = P.astype(BF)
    i_idx = np.arange(128)[:, None]
    c_idx = np.arange(896)[None, :]
    c["c_mask896"] = (i_idx <= c_idx - 384).astype(BF)
    c["c_ones128"] = np.ones((128, 1), dtype=BF)
    c["c_onesb"] = np.ones((1, 512), dtype=BF)
    c["c_onesf"] = np.ones((1, 512), dtype=np.float32)
    sel = np.zeros((16, 256), dtype=np.float32)
    for p in range(128):
        sel[p // 16, p] = 1.0
        sel[8 + p // 16, 128 + p] = 1.0
    c["c_sel"] = sel
    t = np.arange(T, dtype=np.float64)
    f = np.exp(-np.arange(0, 128, 2, dtype=np.float64) * (np.log(10000.0) / 128))
    ang1 = t[None, :] * f[:, None]
    c["c_cos1"] = np.concatenate([np.cos(ang1), np.cos(ang1)], 0).astype(BF)
    c["c_sin1"] = np.concatenate([np.sin(ang1), np.sin(ang1)], 0).astype(BF)
    ang0 = t[None, :] * f[:, None]
    cos0 = np.repeat(np.cos(ang0), 2, axis=0)
    sin0 = np.repeat(np.sin(ang0), 2, axis=0)
    c["c_cos0"] = cos0.astype(BF)
    c["c_sin0"] = sin0.astype(BF)
    gh = np.zeros((96, 96), dtype=np.float32)
    for c1 in range(96):
        for c2 in range(96):
            if c1 // 8 == c2 // 8:
                gh[c1, c2] = 1.0 / 1024.0
    c["c_ghead"] = gh
    al = (np.arange(256, dtype=np.float64) / 255.0)
    c["c_gmid"] = np.concatenate([al, 1.0 - al]).reshape(1, 512).astype(np.float32)
    c["c_gw0"] = np.concatenate([np.ones(256), 1.0 - al]).reshape(1, 512).astype(np.float32)
    c["c_gw7"] = al.reshape(1, 256).astype(np.float32)
    ii = np.arange(128)[:, None]
    cc = np.arange(2944)[None, :]
    Bm = (ii - cc + 384).astype(np.float32)
    c["c_balibi"] = np.minimum(Bm, 0.0)
    c["c_bmaskz"] = (Bm <= 0).astype(BF)
    return c


def host_weights(inp):
    w = {}
    W = np.asarray(inp["L0_ffn_W"], dtype=np.float32)
    Bf = np.asarray(inp["L0_ffn_Bf"], dtype=np.float32)
    A = np.asarray(inp["L0_ffn_A"], dtype=np.float32)
    wblk = lhsT_blocked(np.ascontiguousarray(W.T))
    M = np.ascontiguousarray(Bf.transpose(0, 2, 1).reshape(256, 4096))
    bfblk = lhsT_blocked(M)
    w["l0f_wblk"] = np.ascontiguousarray(np.concatenate([wblk, bfblk], axis=2))
    Ars = np.ascontiguousarray(A.reshape(256, 4096).T)
    w["l0f_ablk"] = lhsT_blocked(Ars)
    w["l0f_bias"] = fm_cols(np.asarray(inp["L0_ffn_bias"], dtype=np.float32))
    w["l0f_coeffs"] = np.asarray(inp["L0_ffn_coeffs"], np.float32).reshape(16, 1)
    w["l0f_scales"] = np.asarray(inp["L0_ffn_scales"], np.float32).reshape(16, 1)
    w["l1f_qblk"] = np.ascontiguousarray(
        np.asarray(inp["L1_ffn_qweight"], np.int32).reshape(NCH, 128, 3072))
    w["l1f_scale"] = fm_cols(np.asarray(inp["L1_ffn_weight_scale"], np.float32)[:, 0])
    w["l1f_zp"] = fm_cols(np.asarray(inp["L1_ffn_weight_zero_point"], np.float32)[:, 0])
    w["l1f_bias"] = fm_cols(np.asarray(inp["L1_ffn_bias"], np.float32))
    w["l1a_qblk"] = lhsT_blocked(np.ascontiguousarray(
        np.asarray(inp["L1_attn_a_Wq"], np.float32).T))
    w["l1a_kblk"] = lhsT_blocked(np.ascontiguousarray(
        np.asarray(inp["L1_attn_a_Wk"], np.float32).T))
    w["l1a_vblk"] = lhsT_blocked(np.ascontiguousarray(
        np.asarray(inp["L1_attn_a_Wv"], np.float32).T))
    w["l1a_oblk"] = lhsT_blocked(np.ascontiguousarray(
        np.asarray(inp["L1_attn_a_Wo"], np.float32).T))
    Wqkv = np.asarray(inp["L1_attn_b_Wqkv"], np.float32)
    w["l1b_qblk"] = lhsT_blocked(np.ascontiguousarray(Wqkv[:, 0:4096]))
    w["l1b_kblk"] = lhsT_blocked(np.ascontiguousarray(Wqkv[:, 4096:4224]))
    w["l1b_vblk"] = lhsT_blocked(np.ascontiguousarray(Wqkv[:, 4224:4352]))
    w["l1b_oblk"] = lhsT_blocked(np.asarray(inp["L1_attn_b_Wo"], np.float32))
    w["l0b_qblk"] = lhsT_blocked(np.ascontiguousarray(
        np.asarray(inp["L0_attn_b_Wq"], np.float32).T))
    w["l0b_kblk"] = lhsT_blocked(np.ascontiguousarray(
        np.asarray(inp["L0_attn_b_Wk"], np.float32).T))
    w["l0b_vblk"] = lhsT_blocked(np.ascontiguousarray(
        np.asarray(inp["L0_attn_b_Wv"], np.float32).T))
    w["l0b_oblk"] = lhsT_blocked(np.ascontiguousarray(
        np.asarray(inp["L0_attn_b_Wo"], np.float32).T))
    w["l0a_bvq"] = np.asarray(inp["L0_attn_a_bv_q"], np.float32)
    w["l0a_bvk"] = np.asarray(inp["L0_attn_a_bv_k"], np.float32)
    w["l0a_bvv"] = np.asarray(inp["L0_attn_a_bv_v"], np.float32)
    w.update(host_constants())
    return w


def weights_meta_from(maps):
    meta = {}
    for k, v in maps.items():
        dt = {np.dtype(np.float32): F32, np.dtype(BF): BF16,
              np.dtype(np.int32): I32}[v.dtype]
        meta[k] = (v.shape, dt)
    return meta


def run_subset(inputs, sublayers, n_cores=8, trace=False, dump=None):
    x = np.asarray(inputs["L0_attn_a_x"], dtype=np.float32)
    common = host_weights(inputs)
    xTs = [np.ascontiguousarray(x[b].T) for b in range(n_cores)]
    maps = [{**common, "xT": xTs[b]} for b in range(n_cores)]
    meta = weights_meta_from(maps[0])
    nc = build_nc(sublayers, meta, dump=dump)
    res = run_bass_kernel_spmd(nc, maps, core_ids=list(range(n_cores)), trace=trace)
    outs = [res.results[b]["out"] for b in range(n_cores)]
    y = np.stack([o.T for o in outs], axis=0).astype(np.float32)
    return y, res


def kernel(**inputs):
    y, _ = run_subset(inputs, SUBLAYERS_ALL, n_cores=8)
    return y
